# version 17
# baseline (speedup 1.0000x reference)
import numpy as np
import ml_dtypes
from contextlib import ExitStack

import concourse.bass as bass
import concourse.mybir as mybir
from concourse.bass_utils import run_bass_kernel_spmd

F32 = mybir.dt.float32
BF16 = mybir.dt.bfloat16
AF = mybir.ActivationFunctionType
ALU = mybir.AluOpType

ENGINES = ("sync", "scalar", "vector", "tensor", "gpsimd")

D = 1024
L = 4096
N_ST = 16
DC = 8
TC = 1024
NTC = L // TC
EPS = 1e-5
HPI = float(np.pi / 2)



class Chan:
    def __init__(self, handle, dma=False):
        self.h = handle
        self.dma = dma
        self.count = 0
        self.amt = 16 if dma else 1


class Buf:
    def __init__(self, ap, name=""):
        self.ap = ap
        self.name = name
        self.last_write = None
        self.readers = []


class Prog:
    def __init__(self, nc):
        self.nc = nc
        self.steps = {e: [] for e in ENGINES}
        self.hwm = {e: {} for e in ENGINES}
        self.all_chans = []

    def chan(self, es, name, dma=False):
        h = es.enter_context(self.nc.semaphore(name))
        c = Chan(h, dma=dma)
        self.all_chans.append(c)
        return c

    def emit(self, engine, fn, reads=(), writes=(), extra_waits=(), inc=None,
             war=()):
        need = {}

        def add(cc):
            if cc is None:
                return
            ch, v = cc
            if v <= 0:
                return
            if need.get(ch, 0) < v:
                need[ch] = v

        for b in reads:
            add(b.last_write)
        for b in list(writes) + list(war):
            add(b.last_write)
            for r in b.readers:
                add(r)
        for w in extra_waits:
            add(w if isinstance(w, tuple) else (w, w.count))

        if inc is not None and inc.dma:
            add((inc, inc.count))
        hw = self.hwm[engine]
        waits = []
        for ch, v in need.items():
            if hw.get(ch, 0) < v:
                waits.append((ch, v))
                hw[ch] = v
        if inc is not None:
            inc.count += inc.amt
        self.steps[engine].append((fn, waits, inc))
        tag = (inc, inc.count) if inc is not None else None
        for b in writes:
            b.last_write = tag
            b.readers = []
        if tag is not None:
            for b in reads:
                b.readers.append(tag)

    def build(self, block):
        def mk(engine):
            def run(eng):
                for fn, waits, inc in self.steps[engine]:
                    for ch, v in waits:
                        eng.wait_ge(ch.h, v)
                    if fn is not None:
                        ins = fn(eng)
                        if inc is not None:
                            assert ins is not None
                            ins.then_inc(inc.h, inc.amt)
                    elif inc is not None:
                        eng.sem_inc(inc.h, inc.amt)
            return run
        block.sync(mk("sync"))
        block.scalar(mk("scalar"))
        block.vector(mk("vector"))
        block.tensor(mk("tensor"))
        block.gpsimd(mk("gpsimd"))



def build_program():
    nc = bass.Bass("TRN2", target_bir_lowering=False)

    xT = nc.dram_tensor("xT", [D, L], F32, kind="ExternalInput").ap()
    A_log_r = nc.dram_tensor("A_log_r", [128, DC * N_ST], F32, kind="ExternalInput").ap()
    Dp_c = nc.dram_tensor("Dp_c", [128, DC], F32, kind="ExternalInput").ap()
    Wxp = nc.dram_tensor("Wxp", [D, D + 2 * N_ST], F32, kind="ExternalInput").ap()
    bxp_c = nc.dram_tensor("bxp_c", [128, DC], F32, kind="ExternalInput").ap()
    bout_row = nc.dram_tensor("bout_row", [1, D], F32, kind="ExternalInput").ap()
    bbc_c = nc.dram_tensor("bbc_c", [N_ST, 2], F32, kind="ExternalInput").ap()
    Wout = nc.dram_tensor("Wout", [D, D], F32, kind="ExternalInput").ap()
    bout_c = nc.dram_tensor("bout_c", [128, DC], F32, kind="ExternalInput").ap()
    theta = nc.dram_tensor("theta", [D, D], F32, kind="ExternalInput").ap()
    phi = nc.dram_tensor("phi", [D, D], F32, kind="ExternalInput").ap()
    theta_n = nc.dram_tensor("theta_n", [D, D], F32, kind="ExternalInput").ap()
    phi_n = nc.dram_tensor("phi_n", [D, D], F32, kind="ExternalInput").ap()
    c_col = nc.dram_tensor("c_col", [128, DC], F32, kind="ExternalInput").ap()
    wmod = nc.dram_tensor("wmod", [D, 6 * D], F32, kind="ExternalInput").ap()

    outT = nc.dram_tensor("outT", [D, L], F32, kind="ExternalOutput").ap()

    wxp_scr = nc.dram_tensor("wxp_scr", [8, 128, DC, 128], BF16).ap()
    wxp8_scr = nc.dram_tensor("wxp8_scr", [128, DC, 32], BF16).ap()
    wout_scr = nc.dram_tensor("wout_scr", [DC, 128, DC, 128], BF16).ap()
    wc_scr = nc.dram_tensor("wc_scr", [DC, 128, DC, 128], BF16).ap()
    ws_scr = nc.dram_tensor("ws_scr", [DC, 128, DC, 128], BF16).ap()

    es = ExitStack()
    with es:
        p = Prog(nc)

        def sb(name, shape, dt):
            return es.enter_context(nc.sbuf_tensor(name, shape, dt))

        xch = [sb("xch0", [128, DC * TC], BF16), sb("xch1", [128, DC * TC], BF16)]
        dl = [sb("dl0", [128, DC * TC], BF16), sb("dl1", [128, DC * TC], BF16)]
        yt = [sb("yt0", [128, DC * TC], BF16), sb("yt1", [128, DC * TC], BF16)]
        r1t = sb("r1t", [128, DC * TC], BF16)
        h2t = sb("h2t", [128, DC * TC], BF16)
        wxr = sb("wxr", [128, 2 * DC * 128], BF16)
        wxr8 = sb("wxr8", [128, DC * 32], BF16)
        wor = sb("wor", [128, 2 * DC * 128], BF16)
        wcr = sb("wcr", [128, 2 * DC * 128], BF16)
        wsr = sb("wsr", [128, 2 * DC * 128], BF16)
        bc_sb = sb("bc_sb", [N_ST, 4 * TC], BF16)
        selB = sb("selB", [N_ST, N_ST * 128], BF16)
        Bbt = sb("Bbt", [128, 2 * TC], BF16)
        Cbt = sb("Cbt", [128, 2 * TC], BF16)
        abr = sb("abr", [128, 2 * TC], BF16)
        bxr = sb("bxr", [128, 2 * TC], BF16)
        hri = sb("hri", [128, 2 * TC], BF16)
        zri = sb("zri", [128, 2 * TC], BF16)
        stt = sb("stt", [128, 128], BF16)
        ab1 = sb("ab1", [128, 2 * TC], BF16)
        sqr = sb("sqr", [128, 512], BF16)
        magb = sb("magb", [128, 512], BF16)
        outs = sb("outs", [128, TC], F32)
        rowf = sb("rowf", [1, 3 * TC], F32)
        cons = sb("cons", [128, 32], F32)
        cbcolb = sb("cbcolb", [128, DC], BF16)
        ones1f = sb("ones1f", [128, 1], F32)
        a_sc = sb("a_sc", [128, 128], F32)
        cols = sb("cols", [128, 64], F32)
        sio = sb("sio", [128, DC], F32)
        siob = sb("siob", [128, DC], BF16)
        ones1 = sb("ones1", [128, 1], BF16)
        onesw = sb("onesw", [1, 512], BF16)
        ones_rf = sb("ones_rf", [1, 128], F32)

        psA = es.enter_context(nc.psum_tensor("psA", [128, TC], F32))
        psB = es.enter_context(nc.psum_tensor("psB", [128, TC], F32))
        psC = es.enter_context(nc.psum_tensor("psC", [128, TC], F32))
        psD = es.enter_context(nc.psum_tensor("psD", [128, TC], F32))

        d_g = p.chan(es, "d_g", dma=True)
        d_s = p.chan(es, "d_s", dma=True)
        d_o = p.chan(es, "d_o", dma=True)
        cv = p.chan(es, "cv")
        cs = p.chan(es, "cs")
        ct = p.chan(es, "ct")
        cp = p.chan(es, "cp")

        block = es.enter_context(nc.Block())

        ch_b = [[Buf(xch[par][:, mc * TC:(mc + 1) * TC], f"ch{par}_{mc}")
                 for mc in range(DC)] for par in range(2)]
        dl_b = [[Buf(dl[par][:, mc * TC:(mc + 1) * TC], f"dl{par}_{mc}")
                 for mc in range(DC)] for par in range(2)]
        yt_b = [[Buf(yt[par][:, dc * TC:(dc + 1) * TC], f"y{par}_{dc}")
                 for dc in range(DC)] for par in range(2)]
        r1_b = [Buf(r1t[:, mc * TC:(mc + 1) * TC], f"r1_{mc}") for mc in range(DC)]
        h2_b = [Buf(h2t[:, mc * TC:(mc + 1) * TC], f"h2_{mc}") for mc in range(DC)]
        wxr_b = [Buf(wxr[:, i * DC * 128:(i + 1) * DC * 128], f"wxr{i}") for i in range(2)]
        wxr8_b = Buf(wxr8[:], "wxr8")
        wor_b = [Buf(wor[:, i * DC * 128:(i + 1) * DC * 128], f"wor{i}") for i in range(2)]
        wcr_b = [Buf(wcr[:, i * DC * 128:(i + 1) * DC * 128], f"wcr{i}") for i in range(2)]
        wsr_b = [Buf(wsr[:, i * DC * 128:(i + 1) * DC * 128], f"wsr{i}") for i in range(2)]
        bc_b = [Buf(bc_sb[:, i * 2 * TC:(i + 1) * 2 * TC], f"bc{i}") for i in range(2)]
        selB_b = Buf(selB[:], "selB")
        Bb_b = [Buf(Bbt[:, i * TC:(i + 1) * TC], f"Bb{i}") for i in range(2)]
        Cb_b = [Buf(Cbt[:, i * TC:(i + 1) * TC], f"Cb{i}") for i in range(2)]
        abr_b = [Buf(abr[:, i * TC:(i + 1) * TC], f"abr{i}") for i in range(2)]
        bxr_b = [Buf(bxr[:, i * TC:(i + 1) * TC], f"bxr{i}") for i in range(2)]
        hr_b = [Buf(hri[:, i * TC:(i + 1) * TC], f"hr{i}") for i in range(2)]
        zr_b = [Buf(zri[:, i * TC:(i + 1) * TC], f"zr{i}") for i in range(2)]
        stt_b = [Buf(stt[:, i:i + 1], f"st{i}") for i in range(128)]
        stt_all = Buf(stt[:], "stt_all")
        ab1_b = Buf(ab1[:], "ab1")
        sqr_b = [Buf(sqr[:], "sqr")] * 2
        magb_b = [Buf(magb[:], "magb")] * 2
        outs_b = [Buf(outs[:, i * 512:(i + 1) * 512], f"outs{i}") for i in range(2)]
        rowf_b = Buf(rowf[:], "rowf")
        cons_b = Buf(cons[:], "cons")
        asc_b = Buf(a_sc[:], "a_sc")
        cols_b = Buf(cols[:], "cols")
        sio_b = Buf(sio[:], "sio")
        siob_b = Buf(siob[:], "siob")
        ones_b = Buf(ones1[:], "ones1")
        onesw_b = Buf(onesw[:], "onesw")
        onesrf_b = Buf(ones_rf[:], "ones_rf")
        cbcolb_b = Buf(cbcolb[:], "cbcolb")
        ones1f_b = Buf(ones1f[:], "ones1f")
        mps = outs[:, 0:48]
        mps_b = Buf(mps, "mps")
        psA_b = Buf(psA[:], "psA")
        psB_b = Buf(psB[:], "psB")
        psC_b = Buf(psC[:], "psC")
        psD_re = Buf(psD[:, 0:512], "psD_re")
        psD_im = Buf(psD[:, 512:1024], "psD_im")

        p.emit("sync", lambda s: s.dma_start(out=a_sc[:], in_=A_log_r),
               writes=[asc_b], inc=d_s)
        p.emit("scalar", lambda s: s.activation(a_sc[:], a_sc[:], AF.Exp),
               reads=[asc_b], writes=[asc_b], inc=cs)
        p.emit("vector", lambda v: v.tensor_scalar_mul(a_sc[:], a_sc[:], -1.0),
               reads=[asc_b], writes=[asc_b], inc=cv)

        p.emit("sync", lambda s: s.dma_start(out=cons[:, 0:DC], in_=Dp_c),
               writes=[cons_b], inc=d_s)
        p.emit("sync", lambda s: s.dma_start(out=cons[:, 8:8 + DC], in_=bxp_c),
               writes=[cons_b], inc=d_s)
        p.emit("sync", lambda s: s.dma_start(out=cons[:, 16:16 + DC], in_=bout_c),
               writes=[cons_b], inc=d_s)
        p.emit("sync", lambda s: s.dma_start(out=cons[0:N_ST, 26:28], in_=bbc_c),
               writes=[cons_b], inc=d_s)
        p.emit("vector", lambda v: v.memset(cons[:, 25:26], HPI),
               writes=[cons_b], inc=cv)
        p.emit("sync", lambda s: s.dma_start(out=sio[:], in_=c_col),
               writes=[sio_b], inc=d_s)
        p.emit("scalar", lambda s: s.activation(sio[:], sio[:], AF.Silu),
               reads=[sio_b], writes=[sio_b], inc=cs)
        p.emit("vector", lambda v: v.tensor_copy(siob[:], sio[:]),
               reads=[sio_b], writes=[siob_b], inc=cv)

        p.emit("vector", lambda v: v.memset(ones1[:], 1.0), writes=[ones_b], inc=cv)
        p.emit("vector", lambda v: v.memset(onesw[:], 1.0), writes=[onesw_b], inc=cv)
        p.emit("vector", lambda v: v.memset(ones_rf[:], 1.0), writes=[onesrf_b], inc=cv)
        p.emit("vector", lambda v: v.memset(ones1f[:], 1.0), writes=[ones1f_b], inc=cv)
        p.emit("vector", lambda v: v.memset(stt[:], 0.0),
               writes=[stt_all] + stt_b, inc=cv)
        p.emit("vector", lambda v: v.memset(selB[:], 0.0), writes=[selB_b], inc=cv)
        for n in range(N_ST):
            p.emit("sync", lambda s, n=n: s.dma_start(
                out=selB[n:n + 1, n * 128:(n + 1) * 128],
                in_=onesw[0:1, 0:128]),
                reads=[onesw_b], writes=[selB_b], inc=d_s)

        p.emit("vector", lambda v: v.memset(mps[:], 0.0), writes=[mps_b], inc=cv)
        for kc in range(DC):
            p.emit("gpsimd", lambda g, kc=kc: g.dma_start(
                out=h2t[:, 0:6 * D], in_=wmod[kc * 128:(kc + 1) * 128, :]),
                writes=h2_b, inc=d_g)
            for j in range(48):
                grp, dcj = j // DC, j % DC
                last = (j == 47)
                p.emit("tensor", lambda t, kc=kc, j=j, grp=grp, dcj=dcj:
                       t.matmul(psC[:, j:j + 1],
                                h2t[:, grp * D + dcj * 128:grp * D + (dcj + 1) * 128],
                                siob[:, kc:kc + 1],
                                start=True, stop=True),
                       reads=h2_b + [siob_b], writes=[psC_b] if last else (),
                       war=() if j else [psC_b],
                       inc=ct if last else None)
            p.emit("vector", lambda v: v.tensor_add(mps[:], mps[:], psC[:, 0:48]),
                   reads=[psC_b], writes=[mps_b], inc=cv)

        p.emit("vector", lambda v: v.tensor_copy(cols[:, 0:8], mps[:, 0:8]),
               reads=[mps_b], writes=[cols_b], inc=cv)
        p.emit("vector", lambda v: v.tensor_scalar_add(cols[:, 8:16], mps[:, 8:16], 1.0),
               reads=[mps_b], writes=[cols_b], inc=cv)
        p.emit("vector", lambda v: v.tensor_copy(cols[:, 24:32], mps[:, 24:32]),
               reads=[mps_b], writes=[cols_b], inc=cv)
        p.emit("vector", lambda v: v.tensor_scalar_add(cols[:, 32:40], mps[:, 32:40], 1.0),
               reads=[mps_b], writes=[cols_b], inc=cv)
        for gi, base in enumerate((16, 40)):
            dst = cols[:, base:base + 8]
            tmp = cols[:, 48:56]
            src = mps[:, base:base + 8]
            p.emit("vector", lambda v, tmp=tmp, src=src:
                   v.tensor_scalar(tmp, src, 0.0, 2.0, ALU.is_ge, ALU.mult),
                   reads=[mps_b], writes=[cols_b], inc=cv)
            p.emit("vector", lambda v, tmp=tmp: v.tensor_scalar_add(tmp, tmp, -1.0),
                   reads=[cols_b], writes=[cols_b], inc=cv)
            p.emit("scalar", lambda s, dst=dst, src=src: s.activation(dst, src, AF.Abs),
                   reads=[mps_b], writes=[cols_b], inc=cs)
            p.emit("vector", lambda v, dst=dst: v.tensor_scalar_max(dst, dst, 0.25),
                   reads=[cols_b], writes=[cols_b], inc=cv)
            p.emit("vector", lambda v, dst=dst, tmp=tmp: v.tensor_mul(dst, dst, tmp),
                   reads=[cols_b], writes=[cols_b], inc=cv)
        p.emit("vector", lambda v: v.tensor_mul(cols[:, 56:64], cols[:, 16:24],
                                                cons[:, 16:24]),
               reads=[cols_b, cons_b], writes=[cols_b], inc=cv)
        p.emit("vector", lambda v: v.tensor_copy(cbcolb[:], mps[:, 32:40]),
               reads=[mps_b], writes=[cbcolb_b], inc=cv)
        for (lo, dst_col) in ((24, 28), (32, 29)):
            p.emit("tensor", lambda t, lo=lo: t.matmul(
                psC[0:1, 0:8], ones1f[:, 0:1], mps[:, lo:lo + 8],
                start=True, stop=True),
                reads=[mps_b, ones1f_b], writes=[psC_b], war=[psC_b], inc=ct)
            p.emit("vector", lambda v, dst_col=dst_col: v.tensor_reduce(
                cons[0:1, dst_col:dst_col + 1], psC[0:1, 0:8],
                mybir.AxisListType.X, ALU.add),
                reads=[psC_b], writes=[cons_b], inc=cv)
        p.emit("vector", lambda v: v.tensor_scalar_add(
            cons[0:1, 29:30], cons[0:1, 29:30], float(D)),
            reads=[cons_b], writes=[cons_b], inc=cv)

        def preswizzle(w_in, n_mc, scr, scr8=None):
            p.emit("gpsimd", lambda g: g.dma_start(
                out=r1t[:, 0:DC * 1024].rearrange("p (c j) -> p c j", c=DC),
                in_=w_in.rearrange("(c p) j -> p c j", p=128)[:, :, 0:1024]),
                writes=r1_b, inc=d_g)
            for mc in range(n_mc):
                p.emit("sync", lambda s, mc=mc: s.dma_start(
                    out=scr[mc],
                    in_=r1t[:, 0:DC * 1024].rearrange("p (c j) -> p c j", c=DC)
                    [:, :, mc * 128:(mc + 1) * 128]),
                    reads=r1_b, inc=d_s)
            if scr8 is not None:
                p.emit("gpsimd", lambda g: g.dma_start(
                    out=wxr8[:].rearrange("p (c j) -> p c j", c=DC),
                    in_=w_in.rearrange("(c p) j -> p c j", p=128)[:, :, 1024:1056]),
                    writes=[wxr8_b], inc=d_g)
                p.emit("sync", lambda s: s.dma_start(
                    out=scr8, in_=wxr8[:].rearrange("p (c j) -> p c j", c=DC)),
                    reads=[wxr8_b], inc=d_s)

        preswizzle(Wxp, 8, wxp_scr, wxp8_scr)
        preswizzle(Wout, 8, wout_scr)

        def build_trig(w1, w2, scr, bias, minus_one=False):
            p.emit("gpsimd", lambda g: g.dma_start(
                out=r1t[:].rearrange("p (c j) -> p c j", c=DC),
                in_=w1.rearrange("(c p) j -> p c j", p=128)),
                writes=r1_b, inc=d_g)
            p.emit("gpsimd", lambda g: g.dma_start(
                out=h2t[:].rearrange("p (c j) -> p c j", c=DC),
                in_=w2.rearrange("(c p) j -> p c j", p=128)),
                writes=h2_b, inc=d_g)
            p.emit("vector", lambda v: v.tensor_add(r1t[:], r1t[:], h2t[:]),
                   reads=r1_b + h2_b, writes=r1_b, inc=cv)
            if bias is None:
                p.emit("scalar", lambda s: s.activation(h2t[:], r1t[:], AF.Sin),
                       reads=r1_b, writes=h2_b, inc=cs)
            else:
                p.emit("scalar", lambda s: s.activation(h2t[:], r1t[:], AF.Sin,
                                                        bias=bias),
                       reads=r1_b + [cons_b], writes=h2_b, inc=cs)
            if minus_one:
                p.emit("vector", lambda v: v.tensor_scalar_add(h2t[:], h2t[:], -1.0),
                       reads=h2_b, writes=h2_b, inc=cv)
            for mc in range(DC):
                p.emit("sync", lambda s, mc=mc: s.dma_start(
                    out=scr[mc],
                    in_=h2t[:].rearrange("p (c j) -> p c j", c=DC)
                    [:, :, mc * 128:(mc + 1) * 128]),
                    reads=h2_b, inc=d_s)

        build_trig(theta, theta_n, wc_scr, cons[:, 25:26], minus_one=True)
        build_trig(phi, phi_n, ws_scr, None)


        def ln_stats_rows_bcast(src_bufs, sqslot_off, abuf, ab_b, with_s=False):
            th = []

            def sq_unit(dc):
                def f():
                    for hh in range(2):
                        last = (dc == DC - 1 and hh == 1)
                        p.emit("scalar", lambda s, hh=hh: s.activation(
                            sqr[:], src_bufs[dc].ap[:, hh * 512:(hh + 1) * 512],
                            AF.Square),
                            reads=[src_bufs[dc]], writes=[sqr_b[0]], inc=cs)
                        p.emit("tensor", lambda t, hh=hh: t.matmul(
                            psA[0:1, hh * 512:(hh + 1) * 512], ones1[:, 0:1],
                            src_bufs[dc].ap[:, hh * 512:(hh + 1) * 512],
                            start=(dc == 0), stop=(dc == DC - 1)),
                            reads=(list(src_bufs) if last else [src_bufs[dc]]) + [ones_b],
                            writes=[psA_b] if last else (),
                            war=() if (dc or hh) else [psA_b],
                            inc=ct if last else None)
                        p.emit("tensor", lambda t, hh=hh: t.matmul(
                            psB[0:1, hh * 512:(hh + 1) * 512], ones1[:, 0:1],
                            sqr[:],
                            start=(dc == 0), stop=(dc == DC - 1)),
                            reads=[sqr_b[0], ones_b],
                            writes=[psB_b] if last else (),
                            war=() if (dc or hh) else [psB_b],
                            inc=ct)
                        if with_s:
                            p.emit("tensor", lambda t, hh=hh: t.matmul(
                                psB[32:33, hh * 512:(hh + 1) * 512],
                                cbcolb[:, dc:dc + 1],
                                src_bufs[dc].ap[:, hh * 512:(hh + 1) * 512],
                                start=(dc == 0), stop=(dc == DC - 1)),
                                reads=[src_bufs[dc], cbcolb_b],
                                writes=[psB_b] if last else (),
                                inc=ct if last else None)
                return f

            for dc in range(DC):
                th.append(sq_unit(dc))

            def rows():
                p.emit("vector", lambda v: v.tensor_scalar_mul(
                    rowf[0:1, 0:TC], psA[0:1, :], 1.0 / D),
                    reads=[psA_b], writes=[rowf_b], inc=cv)
                p.emit("scalar", lambda s: s.activation(
                    psA[0:1, :], rowf[0:1, 0:TC], AF.Square),
                    reads=[rowf_b], writes=[psA_b], inc=cs)
                p.emit("vector", lambda v: v.tensor_scalar(
                    rowf[0:1, TC:2 * TC], psB[0:1, :], 1.0 / D, EPS,
                    ALU.mult, ALU.add),
                    reads=[psB_b], writes=[rowf_b], inc=cv)
                p.emit("vector", lambda v: v.tensor_sub(
                    rowf[0:1, TC:2 * TC], rowf[0:1, TC:2 * TC], psA[0:1, :]),
                    reads=[rowf_b, psA_b], writes=[rowf_b], inc=cv)
                p.emit("scalar", lambda s: s.activation(
                    rowf[0:1, TC:2 * TC], rowf[0:1, TC:2 * TC], AF.Sqrt),
                    reads=[rowf_b], writes=[rowf_b], inc=cs)
                p.emit("vector", lambda v: v.reciprocal(
                    rowf[0:1, TC:2 * TC], rowf[0:1, TC:2 * TC]),
                    reads=[rowf_b], writes=[rowf_b], inc=cv)
                if with_s:
                    p.emit("vector", lambda v: v.scalar_tensor_tensor(
                        rowf[0:1, 2 * TC:3 * TC], rowf[0:1, 0:TC], float(D),
                        psB[32:33, :], ALU.mult, ALU.add),
                        reads=[rowf_b, psB_b], writes=[rowf_b], inc=cv)
                    p.emit("vector", lambda v: v.tensor_mul(
                        rowf[0:1, 2 * TC:3 * TC], rowf[0:1, 2 * TC:3 * TC],
                        rowf[0:1, TC:2 * TC]),
                        reads=[rowf_b], writes=[rowf_b], inc=cv)
                p.emit("vector", lambda v: v.scalar_tensor_tensor(
                    rowf[0:1, 0:TC], rowf[0:1, 0:TC], -1.0,
                    rowf[0:1, TC:2 * TC], ALU.mult, ALU.mult),
                    reads=[rowf_b], writes=[rowf_b], inc=cv)
                if with_s:
                    p.emit("vector", lambda v: v.scalar_tensor_tensor(
                        rowf[0:1, 2 * TC:3 * TC], rowf[0:1, 0:TC],
                        cons[0:1, 29:30], rowf[0:1, 2 * TC:3 * TC],
                        ALU.mult, ALU.add),
                        reads=[rowf_b, cons_b], writes=[rowf_b], inc=cv)
                    p.emit("vector", lambda v: v.tensor_single_scalar(
                        rowf[0:1, 2 * TC:3 * TC], rowf[0:1, 2 * TC:3 * TC],
                        cons[0:1, 28:29], ALU.add),
                        reads=[cons_b], writes=[rowf_b], inc=cv)
            th.append(rows)

            def bcast():
                for part, off in ((TC, 0), (0, TC)):
                    for hh in range(2):
                        last = (hh == 1)
                        p.emit("tensor", lambda t, part=part, hh=hh: t.matmul(
                            psA[:, hh * 512:(hh + 1) * 512], ones_rf[0:1, :],
                            rowf[0:1, part + hh * 512:part + (hh + 1) * 512],
                            start=True, stop=True),
                            reads=[rowf_b, onesrf_b], writes=[psA_b] if last else (),
                            war=() if hh else [psA_b],
                            inc=ct if last else None)
                    p.emit("scalar", lambda s, off=off: s.activation(
                        abuf[:, off:off + TC], psA[:, :], AF.Copy),
                        reads=[psA_b], writes=[ab_b], inc=cs)
            th.append(bcast)
            return th

        def stage_A(tc):
            par = tc % 2
            th = []
            t0 = tc * TC

            def load():
                p.emit("gpsimd", lambda g: g.dma_start(
                    out=xch[par][:].rearrange("p (c j) -> p c j", c=DC),
                    in_=xT.rearrange("(c p) t -> p c t", p=128)[:, :, t0:t0 + TC]),
                    writes=ch_b[par], inc=d_g)
            th.append(load)
            th += ln_stats_rows_bcast(ch_b[par], 0, ab1, ab1_b)

            def h1_unit(dc):
                def f():
                    sl = ch_b[par][dc]
                    p.emit("vector", lambda v: v.tensor_mul(
                        sl.ap, sl.ap, ab1[:, 0:TC]),
                        reads=[ab1_b], writes=[sl], inc=cv)
                    p.emit("vector", lambda v: v.tensor_add(
                        sl.ap, sl.ap, ab1[:, TC:2 * TC]),
                        reads=[ab1_b], writes=[sl], inc=cv)
                    p.emit("vector", lambda v: v.tensor_scalar(
                        sl.ap, sl.ap, cols[:, 8 + dc:9 + dc],
                        cols[:, dc:dc + 1], ALU.mult, ALU.add),
                        reads=[cols_b], writes=[sl], inc=cv)
                return f
            for dc in range(DC):
                th.append(h1_unit(dc))
            return th

        def stage_B(tc):
            par = tc % 2
            th = []

            def mc8():
                p.emit("sync", lambda s: s.dma_start(
                    out=wxr8[:].rearrange("p (c j) -> p c j", c=DC),
                    in_=wxp8_scr), writes=[wxr8_b], inc=d_s)
                for half, ps, ps_b in ((0, psB, psB_b), (1, psC, psC_b)):
                    for kc in range(DC):
                        for hh in range(2):
                            last = (kc == DC - 1 and hh == 1)
                            p.emit("tensor", lambda t, kc=kc, hh=hh, half=half,
                                   ps=ps: t.matmul(
                                ps[0:N_ST, hh * 512:(hh + 1) * 512],
                                wxr8[:, kc * 32 + half * 16:kc * 32 + half * 16 + 16],
                                xch[par][:, kc * TC + hh * 512:kc * TC + (hh + 1) * 512],
                                start=(kc == 0), stop=(kc == DC - 1)),
                                reads=[wxr8_b] + (ch_b[par] if last else [ch_b[par][kc]]),
                                writes=[ps_b] if last else (),
                                war=() if (kc or hh) else [ps_b],
                                inc=ct if last else None)
                    p.emit("vector", lambda v, half=half, ps=ps: v.tensor_single_scalar(
                        bc_sb[:, par * 2 * TC + half * TC:par * 2 * TC + (half + 1) * TC],
                        ps[0:N_ST, :], cons[0:N_ST, 26 + half:27 + half], ALU.add),
                        reads=[ps_b, cons_b], writes=[bc_b[par]], inc=cv)
            th.append(mc8)

            def mc_unit(mc):
                def f():
                    wr = wxr_b[mc % 2]
                    p.emit("sync", lambda s: s.dma_start(
                        out=wr.ap.rearrange("p (c j) -> p c j", c=DC),
                        in_=wxp_scr[mc]), writes=[wr], inc=d_s)
                    for kc in range(DC):
                        for hh in range(2):
                            last = (kc == DC - 1 and hh == 1)
                            p.emit("tensor", lambda t, kc=kc, hh=hh: t.matmul(
                                psB[:, hh * 512:(hh + 1) * 512],
                                wr.ap[:, kc * 128:(kc + 1) * 128],
                                xch[par][:, kc * TC + hh * 512:kc * TC + (hh + 1) * 512],
                                start=(kc == 0), stop=(kc == DC - 1)),
                                reads=[wr] + (ch_b[par] if last else [ch_b[par][kc]]),
                                writes=[psB_b] if last else (),
                                war=() if (kc or hh) else [psB_b],
                                inc=ct if last else None)
                    for hh in range(2):
                        p.emit("scalar", lambda s, hh=hh: s.activation(
                            magb[:], psB[:, hh * 512:(hh + 1) * 512], AF.Exp,
                            bias=cons[:, 8 + mc:9 + mc]),
                            reads=[psB_b, cons_b], writes=[magb_b[0]], inc=cs)
                        p.emit("scalar", lambda s, hh=hh: s.activation(
                            dl[par][:, mc * TC + hh * 512:mc * TC + (hh + 1) * 512],
                            magb[:], AF.Ln, bias=1.0),
                            reads=[magb_b[0]], writes=[dl_b[par][mc]], inc=cs)
                    p.emit("scalar", lambda s: s.activation(
                        yt[par][:, mc * TC:(mc + 1) * TC],
                        xch[par][:, mc * TC:(mc + 1) * TC], AF.Copy,
                        scale=cons[:, mc:mc + 1]),
                        reads=[ch_b[par][mc], cons_b], writes=[yt_b[par][mc]], inc=cs)
                return f
            for mc in range(DC):
                th.append(mc_unit(mc))

            def u_unit(mc):
                def f():
                    p.emit("vector", lambda v: v.tensor_mul(
                        xch[par][:, mc * TC:(mc + 1) * TC],
                        dl[par][:, mc * TC:(mc + 1) * TC],
                        xch[par][:, mc * TC:(mc + 1) * TC]),
                        reads=[dl_b[par][mc]], writes=[ch_b[par][mc]], inc=cv)
                return f
            for mc in range(DC):
                th.append(u_unit(mc))
            return th

        def stage_D(tc):
            par = tc % 2
            t0 = tc * TC
            th = []

            def x2load():
                p.emit("gpsimd", lambda g: g.dma_start(
                    out=r1t[:].rearrange("p (c j) -> p c j", c=DC),
                    in_=xT.rearrange("(c p) t -> p c t", p=128)[:, :, t0:t0 + TC]),
                    writes=r1_b, inc=d_g)
                for mc in range(DC):
                    p.emit("vector", lambda v, mc=mc: v.tensor_single_scalar(
                        r1t[:, mc * TC:(mc + 1) * TC],
                        r1t[:, mc * TC:(mc + 1) * TC],
                        cols[:, 56 + mc:57 + mc], ALU.add),
                        reads=[cols_b], writes=[r1_b[mc]], inc=cv)
            th.append(x2load)

            def wout_unit(mc):
                def f():
                    wr = wor_b[mc % 2]
                    p.emit("sync", lambda s: s.dma_start(
                        out=wr.ap.rearrange("p (c j) -> p c j", c=DC),
                        in_=wout_scr[mc]), writes=[wr], inc=d_s)
                    for kc in range(DC):
                        for hh in range(2):
                            last = (kc == DC - 1 and hh == 1)
                            p.emit("tensor", lambda t, kc=kc, hh=hh: t.matmul(
                                psD[:, hh * 512:(hh + 1) * 512],
                                wr.ap[:, kc * 128:(kc + 1) * 128],
                                yt[par][:, kc * TC + hh * 512:kc * TC + (hh + 1) * 512],
                                start=(kc == 0), stop=(kc == DC - 1)),
                                reads=[wr] + (yt_b[par] if last else [yt_b[par][kc]]),
                                writes=[psD_re, psD_im] if last else (),
                                war=() if (kc or hh) else [psD_re, psD_im],
                                inc=ct if last else None)
                    p.emit("vector", lambda v: v.scalar_tensor_tensor(
                        r1t[:, mc * TC:(mc + 1) * TC], psD[:, :],
                        cols[:, 16 + mc:17 + mc],
                        r1t[:, mc * TC:(mc + 1) * TC], ALU.mult, ALU.add),
                        reads=[psD_re, psD_im, cols_b], writes=[r1_b[mc]], inc=cv)
                return f
            for mc in range(DC):
                th.append(wout_unit(mc))

            th += ln_stats_rows_bcast(r1_b, 1, ab1, ab1_b, with_s=True)

            def h2_unit(dc):
                def f():
                    p.emit("vector", lambda v: v.tensor_mul(
                        h2t[:, dc * TC:(dc + 1) * TC],
                        r1t[:, dc * TC:(dc + 1) * TC], ab1[:, 0:TC]),
                        reads=[r1_b[dc], ab1_b], writes=[h2_b[dc]], inc=cv)
                    p.emit("vector", lambda v: v.tensor_add(
                        h2t[:, dc * TC:(dc + 1) * TC],
                        h2t[:, dc * TC:(dc + 1) * TC], ab1[:, TC:2 * TC]),
                        reads=[ab1_b], writes=[h2_b[dc]], inc=cv)
                    p.emit("vector", lambda v: v.tensor_scalar(
                        h2t[:, dc * TC:(dc + 1) * TC],
                        h2t[:, dc * TC:(dc + 1) * TC],
                        cols[:, 32 + dc:33 + dc],
                        cols[:, 24 + dc:25 + dc], ALU.mult, ALU.add),
                        reads=[cols_b], writes=[h2_b[dc]], inc=cv)
                return f
            for dc in range(DC):
                th.append(h2_unit(dc))

            def ffn_unit(mc):
                def f():
                    wc_ = wcr_b[mc % 2]
                    ws_ = wsr_b[mc % 2]
                    p.emit("sync", lambda s: s.dma_start(
                        out=wc_.ap.rearrange("p (c j) -> p c j", c=DC),
                        in_=wc_scr[mc]), writes=[wc_], inc=d_s)
                    p.emit("sync", lambda s: s.dma_start(
                        out=ws_.ap.rearrange("p (c j) -> p c j", c=DC),
                        in_=ws_scr[mc]), writes=[ws_], inc=d_s)
                    for hh in range(2):
                        p.emit("tensor", lambda t, hh=hh: t.matmul(
                            psD[:, 0:512], ones_rf[0:1, :],
                            rowf[0:1, 2 * TC + hh * 512:2 * TC + (hh + 1) * 512],
                            start=True, stop=False),
                            reads=[rowf_b, onesrf_b], war=[psD_re], inc=None)
                        for kc in range(DC):
                            last = (kc == DC - 1)
                            p.emit("tensor", lambda t, kc=kc, hh=hh: t.matmul(
                                psD[:, 0:512],
                                wc_.ap[:, kc * 128:(kc + 1) * 128],
                                h2t[:, kc * TC + hh * 512:kc * TC + (hh + 1) * 512],
                                start=False, stop=last),
                                reads=[wc_] + (h2_b if last else [h2_b[kc]]),
                                writes=[psD_re] if last else (),
                                inc=ct if last else None)
                        for kc in range(DC):
                            last = (kc == DC - 1)
                            p.emit("tensor", lambda t, kc=kc, hh=hh: t.matmul(
                                psD[:, 512:1024],
                                ws_.ap[:, kc * 128:(kc + 1) * 128],
                                h2t[:, kc * TC + hh * 512:kc * TC + (hh + 1) * 512],
                                start=(kc == 0), stop=last),
                                reads=[ws_] + (h2_b if last else [h2_b[kc]]),
                                writes=[psD_im] if last else (),
                                war=() if kc else [psD_im],
                                inc=ct if last else None)
                        p.emit("scalar", lambda s: s.activation(
                            psD[:, 0:512], psD[:, 0:512], AF.Square),
                            reads=[psD_re], writes=[psD_re], inc=cs)
                        p.emit("scalar", lambda s, hh=hh: s.activation(
                            outs[:, hh * 512:(hh + 1) * 512], psD[:, 512:1024],
                            AF.Square),
                            reads=[psD_im], writes=[outs_b[hh]], inc=cs)
                        p.emit("vector", lambda v, hh=hh: v.tensor_add(
                            outs[:, hh * 512:(hh + 1) * 512],
                            outs[:, hh * 512:(hh + 1) * 512], psD[:, 0:512]),
                            reads=[outs_b[hh], psD_re], writes=[outs_b[hh]], inc=cv)
                        p.emit("scalar", lambda s, hh=hh: s.activation(
                            magb[:], outs[:, hh * 512:(hh + 1) * 512], AF.Sqrt),
                            reads=[outs_b[hh]], writes=[magb_b[0]], inc=cs)
                        p.emit("vector", lambda v, hh=hh: v.scalar_tensor_tensor(
                            outs[:, hh * 512:(hh + 1) * 512],
                            magb[:],
                            cols[:, 40 + mc:41 + mc],
                            r1t[:, mc * TC + hh * 512:mc * TC + (hh + 1) * 512],
                            ALU.mult, ALU.add),
                            reads=[magb_b[0], cols_b, r1_b[mc]],
                            writes=[outs_b[hh]], inc=cv)
                        p.emit("sync", lambda s, hh=hh: s.dma_start(
                            out=outT[mc * 128:(mc + 1) * 128,
                                     t0 + hh * 512:t0 + (hh + 1) * 512],
                            in_=outs[:, hh * 512:(hh + 1) * 512]),
                            reads=[outs_b[hh]], inc=d_o)
                return f
            for mc in range(DC):
                th.append(ffn_unit(mc))
            return th

        def stage_C(tc, bg):
            par = tc % 2
            bgi = 0
            for n in range(N_ST):
                npar = n % 2
                for (half, dst) in ((0, Bb_b[npar]), (1, Cb_b[npar])):
                    for hh in range(2):
                        last = (hh == 1)
                        p.emit("tensor", lambda t, half=half, hh=hh, n=n: t.matmul(
                            psC[:, hh * 512:(hh + 1) * 512],
                            selB[:, n * 128:(n + 1) * 128],
                            bc_sb[:, par * 2 * TC + half * TC + hh * 512:
                                  par * 2 * TC + half * TC + (hh + 1) * 512],
                            start=True, stop=True),
                            reads=[selB_b, bc_b[par]],
                            writes=[psC_b] if last else (),
                            war=() if hh else [psC_b],
                            inc=ct if last else None)
                    p.emit("scalar", lambda s, dst=dst: s.activation(
                        dst.ap, psC[:, :], AF.Copy),
                        reads=[psC_b], writes=[dst], inc=cs)
                for dc in range(DC):
                    it = n * DC + dc
                    slot = it % 2
                    i_st = dc * N_ST + n
                    ab_, bx_, h_, z_ = abr_b[slot], bxr_b[slot], hr_b[slot], zr_b[slot]
                    p.emit("scalar", lambda s, ab_=ab_, dc=dc, i_st=i_st: s.activation(
                        ab_.ap, dl[par][:, dc * TC:(dc + 1) * TC],
                        AF.Exp, scale=a_sc[:, i_st:i_st + 1]),
                        reads=[dl_b[par][dc], asc_b], writes=[ab_], inc=cs)
                    p.emit("vector", lambda v, bx_=bx_, dc=dc, npar=npar:
                           v.tensor_mul(bx_.ap,
                                        xch[par][:, dc * TC:(dc + 1) * TC],
                                        Bbt[:, npar * TC:(npar + 1) * TC]),
                           reads=[ch_b[par][dc], Bb_b[npar]], writes=[bx_], inc=cv)
                    p.emit("vector", lambda v, bx_=bx_, ab_=ab_, i_st=i_st:
                           v.scalar_tensor_tensor(
                        bx_.ap[:, 0:1], ab_.ap[:, 0:1], stt[:, i_st:i_st + 1],
                        bx_.ap[:, 0:1], ALU.mult, ALU.add),
                        reads=[ab_, stt_b[i_st]], writes=[bx_], inc=cv)
                    p.emit("vector", lambda v, h_=h_, ab_=ab_, bx_=bx_:
                           v.tensor_tensor_scan(h_.ap, ab_.ap, bx_.ap, 0.0,
                                                ALU.mult, ALU.add),
                           reads=[ab_, bx_], writes=[h_], inc=cv)
                    p.emit("gpsimd", lambda g, h_=h_, i_st=i_st: g.tensor_copy(
                        stt[:, i_st:i_st + 1], h_.ap[:, TC - 1:TC]),
                        reads=[h_], writes=[stt_b[i_st]], inc=cp)
                    z_eng = "gpsimd" if (it % 4 == 1) else "vector"
                    z_chan = cp if z_eng == "gpsimd" else cv
                    p.emit(z_eng, lambda e, z_=z_, h_=h_, npar=npar: e.tensor_mul(
                        z_.ap, h_.ap,
                        Cbt[:, npar * TC:(npar + 1) * TC]),
                        reads=[h_, Cb_b[npar]], writes=[z_], inc=z_chan)
                    p.emit("gpsimd", lambda g, z_=z_, dc=dc: g.tensor_add(
                        yt[par][:, dc * TC:(dc + 1) * TC],
                        yt[par][:, dc * TC:(dc + 1) * TC], z_.ap),
                        reads=[z_], writes=[yt_b[par][dc]], inc=cp)
                n_take = (len(bg) * (n + 1)) // N_ST - (len(bg) * n) // N_ST
                for _ in range(n_take):
                    if bgi < len(bg):
                        bg[bgi]()
                        bgi += 1
            while bgi < len(bg):
                bg[bgi]()
                bgi += 1

        for f in stage_A(0):
            f()
        for f in stage_B(0):
            f()
        for tc in range(NTC):
            bg = []
            if tc > 0:
                bg += stage_D(tc - 1)
            if tc + 1 < NTC:
                bg += stage_A(tc + 1)
                bg += stage_B(tc + 1)
            stage_C(tc, bg)
        for f in stage_D(NTC - 1):
            f()

        p.emit("sync", None, extra_waits=[d_o])
        p.build(block)
    return nc


_NC_CACHE = {}


def kernel(L0_attn_x, L0_attn_A_log, L0_attn_Dp, L0_attn_W_xproj, L0_attn_b_xproj,
           L0_attn_W_out, L0_attn_b_out, L0_ffn_theta, L0_ffn_phi,
           L0_ffn_theta_noise, L0_ffn_phi_noise, L0_c, L0_wmod):
    B = L0_attn_x.shape[0]
    f32 = np.float32
    cc = np.ascontiguousarray

    A_log_r = cc(np.asarray(L0_attn_A_log, f32).reshape(DC, 128, N_ST)
                 .transpose(1, 0, 2).reshape(128, DC * N_ST))
    Dp_c = cc(np.asarray(L0_attn_Dp, f32).reshape(DC, 128).T)
    bxp_c = cc(np.asarray(L0_attn_b_xproj[:D], f32).reshape(DC, 128).T)
    _bbc = np.asarray(L0_attn_b_xproj[D:], f32)
    bbc_c = cc(np.stack([_bbc[:N_ST], _bbc[N_ST:]], axis=1))
    bout_c = cc(np.asarray(L0_attn_b_out, f32).reshape(DC, 128).T)

    if "nc" not in _NC_CACHE:
        _NC_CACHE["nc"] = build_program()
    nc = _NC_CACHE["nc"]

    in_maps = []
    for b in range(B):
        in_maps.append(dict(
            xT=cc(np.asarray(L0_attn_x[b], f32).T),
            A_log_r=A_log_r,
            Dp_c=Dp_c,
            Wxp=cc(np.asarray(L0_attn_W_xproj, f32)),
            bxp_c=bxp_c,
            bout_row=cc(np.asarray(L0_attn_b_out, f32).reshape(1, D)),
            bbc_c=bbc_c,
            Wout=cc(np.asarray(L0_attn_W_out, f32)),
            bout_c=bout_c,
            theta=cc(np.asarray(L0_ffn_theta, f32)),
            phi=cc(np.asarray(L0_ffn_phi, f32)),
            theta_n=cc(np.asarray(L0_ffn_theta_noise, f32)),
            phi_n=cc(np.asarray(L0_ffn_phi_noise, f32)),
            c_col=cc(np.asarray(L0_c[b], f32).reshape(DC, 128).T),
            wmod=cc(np.asarray(L0_wmod, f32)),
        ))

    res = run_bass_kernel_spmd(nc, in_maps, core_ids=list(range(B)))
    out = np.empty((B, L, D), np.float32)
    for b in range(B):
        out[b] = res.results[b]["outT"].T
    return out


# version 19
# speedup vs baseline: 1.0003x; 1.0003x over previous
import numpy as np
import ml_dtypes
from contextlib import ExitStack

import concourse.bass as bass
import concourse.mybir as mybir
from concourse.bass_utils import run_bass_kernel_spmd

F32 = mybir.dt.float32
BF16 = mybir.dt.bfloat16
AF = mybir.ActivationFunctionType
ALU = mybir.AluOpType

ENGINES = ("sync", "scalar", "vector", "tensor", "gpsimd")

D = 1024
L = 4096
N_ST = 16
DC = 8
TC = 1024
NTC = L // TC
EPS = 1e-5
HPI = float(np.pi / 2)



class Chan:
    def __init__(self, handle, dma=False):
        self.h = handle
        self.dma = dma
        self.count = 0
        self.amt = 16 if dma else 1


class Buf:
    def __init__(self, ap, name=""):
        self.ap = ap
        self.name = name
        self.last_write = None
        self.readers = []


class Prog:
    def __init__(self, nc):
        self.nc = nc
        self.steps = {e: [] for e in ENGINES}
        self.hwm = {e: {} for e in ENGINES}
        self.all_chans = []

    def chan(self, es, name, dma=False):
        h = es.enter_context(self.nc.semaphore(name))
        c = Chan(h, dma=dma)
        self.all_chans.append(c)
        return c

    def emit(self, engine, fn, reads=(), writes=(), extra_waits=(), inc=None,
             war=()):
        need = {}

        def add(cc):
            if cc is None:
                return
            ch, v = cc
            if v <= 0:
                return
            if need.get(ch, 0) < v:
                need[ch] = v

        for b in reads:
            add(b.last_write)
        for b in list(writes) + list(war):
            add(b.last_write)
            for r in b.readers:
                add(r)
        for w in extra_waits:
            add(w if isinstance(w, tuple) else (w, w.count))

        if inc is not None and inc.dma:
            add((inc, inc.count))
        hw = self.hwm[engine]
        waits = []
        for ch, v in need.items():
            if hw.get(ch, 0) < v:
                waits.append((ch, v))
                hw[ch] = v
        if inc is not None:
            inc.count += inc.amt
        self.steps[engine].append((fn, waits, inc))
        tag = (inc, inc.count) if inc is not None else None
        for b in writes:
            b.last_write = tag
            b.readers = []
        if tag is not None:
            for b in reads:
                b.readers.append(tag)

    def build(self, block):
        def mk(engine):
            def run(eng):
                for fn, waits, inc in self.steps[engine]:
                    for ch, v in waits:
                        eng.wait_ge(ch.h, v)
                    if fn is not None:
                        ins = fn(eng)
                        if inc is not None:
                            assert ins is not None
                            ins.then_inc(inc.h, inc.amt)
                    elif inc is not None:
                        eng.sem_inc(inc.h, inc.amt)
            return run
        block.sync(mk("sync"))
        block.scalar(mk("scalar"))
        block.vector(mk("vector"))
        block.tensor(mk("tensor"))
        block.gpsimd(mk("gpsimd"))



def build_program():
    nc = bass.Bass("TRN2", target_bir_lowering=False)

    xT = nc.dram_tensor("xT", [D, L], F32, kind="ExternalInput").ap()
    A_log_r = nc.dram_tensor("A_log_r", [128, DC * N_ST], F32, kind="ExternalInput").ap()
    Dp_c = nc.dram_tensor("Dp_c", [128, DC], F32, kind="ExternalInput").ap()
    Wxp = nc.dram_tensor("Wxp", [D, D + 2 * N_ST], F32, kind="ExternalInput").ap()
    bxp_c = nc.dram_tensor("bxp_c", [128, DC], F32, kind="ExternalInput").ap()
    bout_row = nc.dram_tensor("bout_row", [1, D], F32, kind="ExternalInput").ap()
    bbc_c = nc.dram_tensor("bbc_c", [N_ST, 2], F32, kind="ExternalInput").ap()
    Wout = nc.dram_tensor("Wout", [D, D], F32, kind="ExternalInput").ap()
    bout_c = nc.dram_tensor("bout_c", [128, DC], F32, kind="ExternalInput").ap()
    theta = nc.dram_tensor("theta", [D, D], F32, kind="ExternalInput").ap()
    phi = nc.dram_tensor("phi", [D, D], F32, kind="ExternalInput").ap()
    theta_n = nc.dram_tensor("theta_n", [D, D], F32, kind="ExternalInput").ap()
    phi_n = nc.dram_tensor("phi_n", [D, D], F32, kind="ExternalInput").ap()
    c_col = nc.dram_tensor("c_col", [128, DC], F32, kind="ExternalInput").ap()
    wmod = nc.dram_tensor("wmod", [D, 6 * D], F32, kind="ExternalInput").ap()

    outT = nc.dram_tensor("outT", [D, L], F32, kind="ExternalOutput").ap()

    wxp_scr = nc.dram_tensor("wxp_scr", [8, 128, DC, 128], BF16).ap()
    wxp8_scr = nc.dram_tensor("wxp8_scr", [128, DC, 32], BF16).ap()
    wout_scr = nc.dram_tensor("wout_scr", [DC, 128, DC, 128], BF16).ap()
    wc_scr = nc.dram_tensor("wc_scr", [DC, 128, DC, 128], BF16).ap()
    ws_scr = nc.dram_tensor("ws_scr", [DC, 128, DC, 128], BF16).ap()

    es = ExitStack()
    with es:
        p = Prog(nc)

        def sb(name, shape, dt):
            return es.enter_context(nc.sbuf_tensor(name, shape, dt))

        xch = [sb("xch0", [128, DC * TC], BF16), sb("xch1", [128, DC * TC], BF16)]
        dl = [sb("dl0", [128, DC * TC], BF16), sb("dl1", [128, DC * TC], BF16)]
        yt = [sb("yt0", [128, DC * TC], BF16), sb("yt1", [128, DC * TC], BF16)]
        r1t = sb("r1t", [128, DC * TC], BF16)
        h2t = sb("h2t", [128, DC * TC], BF16)
        wxr = sb("wxr", [128, 2 * DC * 128], BF16)
        wxr8 = sb("wxr8", [128, DC * 32], BF16)
        wor = sb("wor", [128, 2 * DC * 128], BF16)
        wcr = sb("wcr", [128, 2 * DC * 128], BF16)
        wsr = sb("wsr", [128, 2 * DC * 128], BF16)
        bc_sb = sb("bc_sb", [N_ST, 4 * TC], BF16)
        selB = sb("selB", [N_ST, N_ST * 128], BF16)
        Bbt = sb("Bbt", [128, 2 * TC], BF16)
        Cbt = sb("Cbt", [128, 2 * TC], BF16)
        abr = sb("abr", [128, 2 * TC], BF16)
        bxr = sb("bxr", [128, 2 * TC], BF16)
        hri = sb("hri", [128, 2 * TC], BF16)
        zri = sb("zri", [128, 2 * TC], BF16)
        stt = sb("stt", [128, 128], F32)
        ctmp = sb("ctmp", [128, 2], BF16)
        ab1 = sb("ab1", [128, 2 * TC], BF16)
        sqr = sb("sqr", [128, 512], BF16)
        magb = sb("magb", [128, 512], BF16)
        outs = sb("outs", [128, TC], F32)
        rowf = sb("rowf", [1, 3 * TC], F32)
        cons = sb("cons", [128, 32], F32)
        cbcolb = sb("cbcolb", [128, DC], BF16)
        ones1f = sb("ones1f", [128, 1], F32)
        a_sc = sb("a_sc", [128, 128], F32)
        cols = sb("cols", [128, 64], F32)
        sio = sb("sio", [128, DC], F32)
        siob = sb("siob", [128, DC], BF16)
        ones1 = sb("ones1", [128, 1], BF16)
        onesw = sb("onesw", [1, 512], BF16)
        ones_rf = sb("ones_rf", [1, 128], F32)

        psA = es.enter_context(nc.psum_tensor("psA", [128, TC], F32))
        psB = es.enter_context(nc.psum_tensor("psB", [128, TC], F32))
        psC = es.enter_context(nc.psum_tensor("psC", [128, TC], F32))
        psD = es.enter_context(nc.psum_tensor("psD", [128, TC], F32))

        d_g = p.chan(es, "d_g", dma=True)
        d_s = p.chan(es, "d_s", dma=True)
        d_o = p.chan(es, "d_o", dma=True)
        cv = p.chan(es, "cv")
        cs = p.chan(es, "cs")
        ct = p.chan(es, "ct")
        cp = p.chan(es, "cp")

        block = es.enter_context(nc.Block())

        ch_b = [[Buf(xch[par][:, mc * TC:(mc + 1) * TC], f"ch{par}_{mc}")
                 for mc in range(DC)] for par in range(2)]
        dl_b = [[Buf(dl[par][:, mc * TC:(mc + 1) * TC], f"dl{par}_{mc}")
                 for mc in range(DC)] for par in range(2)]
        yt_b = [[Buf(yt[par][:, dc * TC:(dc + 1) * TC], f"y{par}_{dc}")
                 for dc in range(DC)] for par in range(2)]
        r1_b = [Buf(r1t[:, mc * TC:(mc + 1) * TC], f"r1_{mc}") for mc in range(DC)]
        h2_b = [Buf(h2t[:, mc * TC:(mc + 1) * TC], f"h2_{mc}") for mc in range(DC)]
        wxr_b = [Buf(wxr[:, i * DC * 128:(i + 1) * DC * 128], f"wxr{i}") for i in range(2)]
        wxr8_b = Buf(wxr8[:], "wxr8")
        wor_b = [Buf(wor[:, i * DC * 128:(i + 1) * DC * 128], f"wor{i}") for i in range(2)]
        wcr_b = [Buf(wcr[:, i * DC * 128:(i + 1) * DC * 128], f"wcr{i}") for i in range(2)]
        wsr_b = [Buf(wsr[:, i * DC * 128:(i + 1) * DC * 128], f"wsr{i}") for i in range(2)]
        bc_b = [Buf(bc_sb[:, i * 2 * TC:(i + 1) * 2 * TC], f"bc{i}") for i in range(2)]
        selB_b = Buf(selB[:], "selB")
        Bb_b = [Buf(Bbt[:, i * TC:(i + 1) * TC], f"Bb{i}") for i in range(2)]
        Cb_b = [Buf(Cbt[:, i * TC:(i + 1) * TC], f"Cb{i}") for i in range(2)]
        abr_b = [Buf(abr[:, i * TC:(i + 1) * TC], f"abr{i}") for i in range(2)]
        bxr_b = [Buf(bxr[:, i * TC:(i + 1) * TC], f"bxr{i}") for i in range(2)]
        hr_b = [Buf(hri[:, i * TC:(i + 1) * TC], f"hr{i}") for i in range(2)]
        zr_b = [Buf(zri[:, i * TC:(i + 1) * TC], f"zr{i}") for i in range(2)]
        stt_b = [Buf(stt[:, i:i + 1], f"st{i}") for i in range(128)]
        stt_all = Buf(stt[:], "stt_all")
        ctmp_b = [Buf(ctmp[:, i:i + 1], f"ctmp{i}") for i in range(2)]
        ab1_b = Buf(ab1[:], "ab1")
        sqr_b = [Buf(sqr[:], "sqr")] * 2
        magb_b = [Buf(magb[:], "magb")] * 2
        outs_b = [Buf(outs[:, i * 512:(i + 1) * 512], f"outs{i}") for i in range(2)]
        rowf_b = Buf(rowf[:], "rowf")
        cons_b = Buf(cons[:], "cons")
        asc_b = Buf(a_sc[:], "a_sc")
        cols_b = Buf(cols[:], "cols")
        sio_b = Buf(sio[:], "sio")
        siob_b = Buf(siob[:], "siob")
        ones_b = Buf(ones1[:], "ones1")
        onesw_b = Buf(onesw[:], "onesw")
        onesrf_b = Buf(ones_rf[:], "ones_rf")
        cbcolb_b = Buf(cbcolb[:], "cbcolb")
        ones1f_b = Buf(ones1f[:], "ones1f")
        mps = outs[:, 0:48]
        mps_b = Buf(mps, "mps")
        psA_b = Buf(psA[:], "psA")
        psB_b = Buf(psB[:], "psB")
        psC_b = Buf(psC[:], "psC")
        psD_re = Buf(psD[:, 0:512], "psD_re")
        psD_im = Buf(psD[:, 512:1024], "psD_im")

        p.emit("sync", lambda s: s.dma_start(out=a_sc[:], in_=A_log_r),
               writes=[asc_b], inc=d_s)
        p.emit("scalar", lambda s: s.activation(a_sc[:], a_sc[:], AF.Exp),
               reads=[asc_b], writes=[asc_b], inc=cs)
        p.emit("vector", lambda v: v.tensor_scalar_mul(a_sc[:], a_sc[:], -1.0),
               reads=[asc_b], writes=[asc_b], inc=cv)

        p.emit("sync", lambda s: s.dma_start(out=cons[:, 0:DC], in_=Dp_c),
               writes=[cons_b], inc=d_s)
        p.emit("sync", lambda s: s.dma_start(out=cons[:, 8:8 + DC], in_=bxp_c),
               writes=[cons_b], inc=d_s)
        p.emit("sync", lambda s: s.dma_start(out=cons[:, 16:16 + DC], in_=bout_c),
               writes=[cons_b], inc=d_s)
        p.emit("sync", lambda s: s.dma_start(out=cons[0:N_ST, 26:28], in_=bbc_c),
               writes=[cons_b], inc=d_s)
        p.emit("vector", lambda v: v.memset(cons[:, 25:26], HPI),
               writes=[cons_b], inc=cv)
        p.emit("sync", lambda s: s.dma_start(out=sio[:], in_=c_col),
               writes=[sio_b], inc=d_s)
        p.emit("scalar", lambda s: s.activation(sio[:], sio[:], AF.Silu),
               reads=[sio_b], writes=[sio_b], inc=cs)
        p.emit("vector", lambda v: v.tensor_copy(siob[:], sio[:]),
               reads=[sio_b], writes=[siob_b], inc=cv)

        p.emit("vector", lambda v: v.memset(ones1[:], 1.0), writes=[ones_b], inc=cv)
        p.emit("vector", lambda v: v.memset(onesw[:], 1.0), writes=[onesw_b], inc=cv)
        p.emit("vector", lambda v: v.memset(ones_rf[:], 1.0), writes=[onesrf_b], inc=cv)
        p.emit("vector", lambda v: v.memset(ones1f[:], 1.0), writes=[ones1f_b], inc=cv)
        p.emit("vector", lambda v: v.memset(stt[:], 0.0),
               writes=[stt_all] + stt_b, inc=cv)
        p.emit("vector", lambda v: v.memset(selB[:], 0.0), writes=[selB_b], inc=cv)
        for n in range(N_ST):
            p.emit("sync", lambda s, n=n: s.dma_start(
                out=selB[n:n + 1, n * 128:(n + 1) * 128],
                in_=onesw[0:1, 0:128]),
                reads=[onesw_b], writes=[selB_b], inc=d_s)

        p.emit("vector", lambda v: v.memset(mps[:], 0.0), writes=[mps_b], inc=cv)
        for kc in range(DC):
            p.emit("gpsimd", lambda g, kc=kc: g.dma_start(
                out=h2t[:, 0:6 * D], in_=wmod[kc * 128:(kc + 1) * 128, :]),
                writes=h2_b, inc=d_g)
            for j in range(48):
                grp, dcj = j // DC, j % DC
                last = (j == 47)
                p.emit("tensor", lambda t, kc=kc, j=j, grp=grp, dcj=dcj:
                       t.matmul(psC[:, j:j + 1],
                                h2t[:, grp * D + dcj * 128:grp * D + (dcj + 1) * 128],
                                siob[:, kc:kc + 1],
                                start=True, stop=True),
                       reads=h2_b + [siob_b], writes=[psC_b] if last else (),
                       war=() if j else [psC_b],
                       inc=ct if last else None)
            p.emit("vector", lambda v: v.tensor_add(mps[:], mps[:], psC[:, 0:48]),
                   reads=[psC_b], writes=[mps_b], inc=cv)

        p.emit("vector", lambda v: v.tensor_copy(cols[:, 0:8], mps[:, 0:8]),
               reads=[mps_b], writes=[cols_b], inc=cv)
        p.emit("vector", lambda v: v.tensor_scalar_add(cols[:, 8:16], mps[:, 8:16], 1.0),
               reads=[mps_b], writes=[cols_b], inc=cv)
        p.emit("vector", lambda v: v.tensor_copy(cols[:, 24:32], mps[:, 24:32]),
               reads=[mps_b], writes=[cols_b], inc=cv)
        p.emit("vector", lambda v: v.tensor_scalar_add(cols[:, 32:40], mps[:, 32:40], 1.0),
               reads=[mps_b], writes=[cols_b], inc=cv)
        for gi, base in enumerate((16, 40)):
            dst = cols[:, base:base + 8]
            tmp = cols[:, 48:56]
            src = mps[:, base:base + 8]
            p.emit("vector", lambda v, tmp=tmp, src=src:
                   v.tensor_scalar(tmp, src, 0.0, 2.0, ALU.is_ge, ALU.mult),
                   reads=[mps_b], writes=[cols_b], inc=cv)
            p.emit("vector", lambda v, tmp=tmp: v.tensor_scalar_add(tmp, tmp, -1.0),
                   reads=[cols_b], writes=[cols_b], inc=cv)
            p.emit("scalar", lambda s, dst=dst, src=src: s.activation(dst, src, AF.Abs),
                   reads=[mps_b], writes=[cols_b], inc=cs)
            p.emit("vector", lambda v, dst=dst: v.tensor_scalar_max(dst, dst, 0.25),
                   reads=[cols_b], writes=[cols_b], inc=cv)
            p.emit("vector", lambda v, dst=dst, tmp=tmp: v.tensor_mul(dst, dst, tmp),
                   reads=[cols_b], writes=[cols_b], inc=cv)
        p.emit("vector", lambda v: v.tensor_mul(cols[:, 56:64], cols[:, 16:24],
                                                cons[:, 16:24]),
               reads=[cols_b, cons_b], writes=[cols_b], inc=cv)
        p.emit("vector", lambda v: v.tensor_copy(cbcolb[:], mps[:, 32:40]),
               reads=[mps_b], writes=[cbcolb_b], inc=cv)
        for (lo, dst_col) in ((24, 28), (32, 29)):
            p.emit("tensor", lambda t, lo=lo: t.matmul(
                psC[0:1, 0:8], ones1f[:, 0:1], mps[:, lo:lo + 8],
                start=True, stop=True),
                reads=[mps_b, ones1f_b], writes=[psC_b], war=[psC_b], inc=ct)
            p.emit("vector", lambda v, dst_col=dst_col: v.tensor_reduce(
                cons[0:1, dst_col:dst_col + 1], psC[0:1, 0:8],
                mybir.AxisListType.X, ALU.add),
                reads=[psC_b], writes=[cons_b], inc=cv)
        p.emit("vector", lambda v: v.tensor_scalar_add(
            cons[0:1, 29:30], cons[0:1, 29:30], float(D)),
            reads=[cons_b], writes=[cons_b], inc=cv)

        def preswizzle(w_in, n_mc, scr, scr8=None):
            p.emit("gpsimd", lambda g: g.dma_start(
                out=r1t[:, 0:DC * 1024].rearrange("p (c j) -> p c j", c=DC),
                in_=w_in.rearrange("(c p) j -> p c j", p=128)[:, :, 0:1024]),
                writes=r1_b, inc=d_g)
            for mc in range(n_mc):
                p.emit("sync", lambda s, mc=mc: s.dma_start(
                    out=scr[mc],
                    in_=r1t[:, 0:DC * 1024].rearrange("p (c j) -> p c j", c=DC)
                    [:, :, mc * 128:(mc + 1) * 128]),
                    reads=r1_b, inc=d_s)
            if scr8 is not None:
                p.emit("gpsimd", lambda g: g.dma_start(
                    out=wxr8[:].rearrange("p (c j) -> p c j", c=DC),
                    in_=w_in.rearrange("(c p) j -> p c j", p=128)[:, :, 1024:1056]),
                    writes=[wxr8_b], inc=d_g)
                p.emit("sync", lambda s: s.dma_start(
                    out=scr8, in_=wxr8[:].rearrange("p (c j) -> p c j", c=DC)),
                    reads=[wxr8_b], inc=d_s)

        preswizzle(Wxp, 8, wxp_scr, wxp8_scr)
        preswizzle(Wout, 8, wout_scr)

        def build_trig(w1, w2, scr, bias, minus_one=False):
            p.emit("gpsimd", lambda g: g.dma_start(
                out=r1t[:].rearrange("p (c j) -> p c j", c=DC),
                in_=w1.rearrange("(c p) j -> p c j", p=128)),
                writes=r1_b, inc=d_g)
            p.emit("gpsimd", lambda g: g.dma_start(
                out=h2t[:].rearrange("p (c j) -> p c j", c=DC),
                in_=w2.rearrange("(c p) j -> p c j", p=128)),
                writes=h2_b, inc=d_g)
            p.emit("vector", lambda v: v.tensor_add(r1t[:], r1t[:], h2t[:]),
                   reads=r1_b + h2_b, writes=r1_b, inc=cv)
            if bias is None:
                p.emit("scalar", lambda s: s.activation(h2t[:], r1t[:], AF.Sin),
                       reads=r1_b, writes=h2_b, inc=cs)
            else:
                p.emit("scalar", lambda s: s.activation(h2t[:], r1t[:], AF.Sin,
                                                        bias=bias),
                       reads=r1_b + [cons_b], writes=h2_b, inc=cs)
            if minus_one:
                p.emit("vector", lambda v: v.tensor_scalar_add(h2t[:], h2t[:], -1.0),
                       reads=h2_b, writes=h2_b, inc=cv)
            for mc in range(DC):
                p.emit("sync", lambda s, mc=mc: s.dma_start(
                    out=scr[mc],
                    in_=h2t[:].rearrange("p (c j) -> p c j", c=DC)
                    [:, :, mc * 128:(mc + 1) * 128]),
                    reads=h2_b, inc=d_s)

        build_trig(theta, theta_n, wc_scr, cons[:, 25:26], minus_one=True)
        build_trig(phi, phi_n, ws_scr, None)


        def ln_stats_rows_bcast(src_bufs, sqslot_off, abuf, ab_b, with_s=False):
            th = []

            def sq_unit(dc):
                def f():
                    for hh in range(2):
                        last = (dc == DC - 1 and hh == 1)
                        p.emit("scalar", lambda s, hh=hh: s.activation(
                            sqr[:], src_bufs[dc].ap[:, hh * 512:(hh + 1) * 512],
                            AF.Square),
                            reads=[src_bufs[dc]], writes=[sqr_b[0]], inc=cs)
                        p.emit("tensor", lambda t, hh=hh: t.matmul(
                            psA[0:1, hh * 512:(hh + 1) * 512], ones1[:, 0:1],
                            src_bufs[dc].ap[:, hh * 512:(hh + 1) * 512],
                            start=(dc == 0), stop=(dc == DC - 1)),
                            reads=(list(src_bufs) if last else [src_bufs[dc]]) + [ones_b],
                            writes=[psA_b] if last else (),
                            war=() if (dc or hh) else [psA_b],
                            inc=ct if last else None)
                        p.emit("tensor", lambda t, hh=hh: t.matmul(
                            psB[0:1, hh * 512:(hh + 1) * 512], ones1[:, 0:1],
                            sqr[:],
                            start=(dc == 0), stop=(dc == DC - 1)),
                            reads=[sqr_b[0], ones_b],
                            writes=[psB_b] if last else (),
                            war=() if (dc or hh) else [psB_b],
                            inc=ct)
                        if with_s:
                            p.emit("tensor", lambda t, hh=hh: t.matmul(
                                psB[32:33, hh * 512:(hh + 1) * 512],
                                cbcolb[:, dc:dc + 1],
                                src_bufs[dc].ap[:, hh * 512:(hh + 1) * 512],
                                start=(dc == 0), stop=(dc == DC - 1)),
                                reads=[src_bufs[dc], cbcolb_b],
                                writes=[psB_b] if last else (),
                                inc=ct if last else None)
                return f

            for dc in range(DC):
                th.append(sq_unit(dc))

            def rows():
                p.emit("vector", lambda v: v.tensor_scalar_mul(
                    rowf[0:1, 0:TC], psA[0:1, :], 1.0 / D),
                    reads=[psA_b], writes=[rowf_b], inc=cv)
                p.emit("scalar", lambda s: s.activation(
                    psA[0:1, :], rowf[0:1, 0:TC], AF.Square),
                    reads=[rowf_b], writes=[psA_b], inc=cs)
                p.emit("vector", lambda v: v.tensor_scalar(
                    rowf[0:1, TC:2 * TC], psB[0:1, :], 1.0 / D, EPS,
                    ALU.mult, ALU.add),
                    reads=[psB_b], writes=[rowf_b], inc=cv)
                p.emit("vector", lambda v: v.tensor_sub(
                    rowf[0:1, TC:2 * TC], rowf[0:1, TC:2 * TC], psA[0:1, :]),
                    reads=[rowf_b, psA_b], writes=[rowf_b], inc=cv)
                p.emit("scalar", lambda s: s.activation(
                    rowf[0:1, TC:2 * TC], rowf[0:1, TC:2 * TC], AF.Sqrt),
                    reads=[rowf_b], writes=[rowf_b], inc=cs)
                p.emit("vector", lambda v: v.reciprocal(
                    rowf[0:1, TC:2 * TC], rowf[0:1, TC:2 * TC]),
                    reads=[rowf_b], writes=[rowf_b], inc=cv)
                if with_s:
                    p.emit("vector", lambda v: v.scalar_tensor_tensor(
                        rowf[0:1, 2 * TC:3 * TC], rowf[0:1, 0:TC], float(D),
                        psB[32:33, :], ALU.mult, ALU.add),
                        reads=[rowf_b, psB_b], writes=[rowf_b], inc=cv)
                    p.emit("vector", lambda v: v.tensor_mul(
                        rowf[0:1, 2 * TC:3 * TC], rowf[0:1, 2 * TC:3 * TC],
                        rowf[0:1, TC:2 * TC]),
                        reads=[rowf_b], writes=[rowf_b], inc=cv)
                p.emit("vector", lambda v: v.scalar_tensor_tensor(
                    rowf[0:1, 0:TC], rowf[0:1, 0:TC], -1.0,
                    rowf[0:1, TC:2 * TC], ALU.mult, ALU.mult),
                    reads=[rowf_b], writes=[rowf_b], inc=cv)
                if with_s:
                    p.emit("vector", lambda v: v.scalar_tensor_tensor(
                        rowf[0:1, 2 * TC:3 * TC], rowf[0:1, 0:TC],
                        cons[0:1, 29:30], rowf[0:1, 2 * TC:3 * TC],
                        ALU.mult, ALU.add),
                        reads=[rowf_b, cons_b], writes=[rowf_b], inc=cv)
                    p.emit("vector", lambda v: v.tensor_single_scalar(
                        rowf[0:1, 2 * TC:3 * TC], rowf[0:1, 2 * TC:3 * TC],
                        cons[0:1, 28:29], ALU.add),
                        reads=[cons_b], writes=[rowf_b], inc=cv)
            th.append(rows)

            def bcast():
                for part, off in ((TC, 0), (0, TC)):
                    for hh in range(2):
                        last = (hh == 1)
                        p.emit("tensor", lambda t, part=part, hh=hh: t.matmul(
                            psA[:, hh * 512:(hh + 1) * 512], ones_rf[0:1, :],
                            rowf[0:1, part + hh * 512:part + (hh + 1) * 512],
                            start=True, stop=True),
                            reads=[rowf_b, onesrf_b], writes=[psA_b] if last else (),
                            war=() if hh else [psA_b],
                            inc=ct if last else None)
                    p.emit("scalar", lambda s, off=off: s.activation(
                        abuf[:, off:off + TC], psA[:, :], AF.Copy),
                        reads=[psA_b], writes=[ab_b], inc=cs)
            th.append(bcast)
            return th

        def stage_A(tc):
            par = tc % 2
            th = []
            t0 = tc * TC

            def load():
                p.emit("gpsimd", lambda g: g.dma_start(
                    out=xch[par][:].rearrange("p (c j) -> p c j", c=DC),
                    in_=xT.rearrange("(c p) t -> p c t", p=128)[:, :, t0:t0 + TC]),
                    writes=ch_b[par], inc=d_g)
            th.append(load)
            th += ln_stats_rows_bcast(ch_b[par], 0, ab1, ab1_b)

            def h1_unit(dc):
                def f():
                    sl = ch_b[par][dc]
                    p.emit("vector", lambda v: v.tensor_mul(
                        sl.ap, sl.ap, ab1[:, 0:TC]),
                        reads=[ab1_b], writes=[sl], inc=cv)
                    p.emit("vector", lambda v: v.tensor_add(
                        sl.ap, sl.ap, ab1[:, TC:2 * TC]),
                        reads=[ab1_b], writes=[sl], inc=cv)
                    p.emit("vector", lambda v: v.tensor_scalar(
                        sl.ap, sl.ap, cols[:, 8 + dc:9 + dc],
                        cols[:, dc:dc + 1], ALU.mult, ALU.add),
                        reads=[cols_b], writes=[sl], inc=cv)
                return f
            for dc in range(DC):
                th.append(h1_unit(dc))
            return th

        def stage_B(tc):
            par = tc % 2
            th = []

            def mc8():
                p.emit("sync", lambda s: s.dma_start(
                    out=wxr8[:].rearrange("p (c j) -> p c j", c=DC),
                    in_=wxp8_scr), writes=[wxr8_b], inc=d_s)
                for half, ps, ps_b in ((0, psB, psB_b), (1, psC, psC_b)):
                    for kc in range(DC):
                        for hh in range(2):
                            last = (kc == DC - 1 and hh == 1)
                            p.emit("tensor", lambda t, kc=kc, hh=hh, half=half,
                                   ps=ps: t.matmul(
                                ps[0:N_ST, hh * 512:(hh + 1) * 512],
                                wxr8[:, kc * 32 + half * 16:kc * 32 + half * 16 + 16],
                                xch[par][:, kc * TC + hh * 512:kc * TC + (hh + 1) * 512],
                                start=(kc == 0), stop=(kc == DC - 1)),
                                reads=[wxr8_b] + (ch_b[par] if last else [ch_b[par][kc]]),
                                writes=[ps_b] if last else (),
                                war=() if (kc or hh) else [ps_b],
                                inc=ct if last else None)
                    p.emit("vector", lambda v, half=half, ps=ps: v.tensor_single_scalar(
                        bc_sb[:, par * 2 * TC + half * TC:par * 2 * TC + (half + 1) * TC],
                        ps[0:N_ST, :], cons[0:N_ST, 26 + half:27 + half], ALU.add),
                        reads=[ps_b, cons_b], writes=[bc_b[par]], inc=cv)
            th.append(mc8)

            def mc_unit(mc):
                def f():
                    wr = wxr_b[mc % 2]
                    p.emit("sync", lambda s: s.dma_start(
                        out=wr.ap.rearrange("p (c j) -> p c j", c=DC),
                        in_=wxp_scr[mc]), writes=[wr], inc=d_s)
                    for kc in range(DC):
                        for hh in range(2):
                            last = (kc == DC - 1 and hh == 1)
                            p.emit("tensor", lambda t, kc=kc, hh=hh: t.matmul(
                                psB[:, hh * 512:(hh + 1) * 512],
                                wr.ap[:, kc * 128:(kc + 1) * 128],
                                xch[par][:, kc * TC + hh * 512:kc * TC + (hh + 1) * 512],
                                start=(kc == 0), stop=(kc == DC - 1)),
                                reads=[wr] + (ch_b[par] if last else [ch_b[par][kc]]),
                                writes=[psB_b] if last else (),
                                war=() if (kc or hh) else [psB_b],
                                inc=ct if last else None)
                    for hh in range(2):
                        p.emit("scalar", lambda s, hh=hh: s.activation(
                            magb[:], psB[:, hh * 512:(hh + 1) * 512], AF.Exp,
                            bias=cons[:, 8 + mc:9 + mc]),
                            reads=[psB_b, cons_b], writes=[magb_b[0]], inc=cs)
                        p.emit("scalar", lambda s, hh=hh: s.activation(
                            dl[par][:, mc * TC + hh * 512:mc * TC + (hh + 1) * 512],
                            magb[:], AF.Ln, bias=1.0),
                            reads=[magb_b[0]], writes=[dl_b[par][mc]], inc=cs)
                    p.emit("scalar", lambda s: s.activation(
                        yt[par][:, mc * TC:(mc + 1) * TC],
                        xch[par][:, mc * TC:(mc + 1) * TC], AF.Copy,
                        scale=cons[:, mc:mc + 1]),
                        reads=[ch_b[par][mc], cons_b], writes=[yt_b[par][mc]], inc=cs)
                return f
            for mc in range(DC):
                th.append(mc_unit(mc))

            def u_unit(mc):
                def f():
                    p.emit("vector", lambda v: v.tensor_mul(
                        xch[par][:, mc * TC:(mc + 1) * TC],
                        dl[par][:, mc * TC:(mc + 1) * TC],
                        xch[par][:, mc * TC:(mc + 1) * TC]),
                        reads=[dl_b[par][mc]], writes=[ch_b[par][mc]], inc=cv)
                return f
            for mc in range(DC):
                th.append(u_unit(mc))
            return th

        def stage_D(tc):
            par = tc % 2
            t0 = tc * TC
            th = []

            def x2load():
                p.emit("gpsimd", lambda g: g.dma_start(
                    out=r1t[:].rearrange("p (c j) -> p c j", c=DC),
                    in_=xT.rearrange("(c p) t -> p c t", p=128)[:, :, t0:t0 + TC]),
                    writes=r1_b, inc=d_g)
                for mc in range(DC):
                    p.emit("vector", lambda v, mc=mc: v.tensor_single_scalar(
                        r1t[:, mc * TC:(mc + 1) * TC],
                        r1t[:, mc * TC:(mc + 1) * TC],
                        cols[:, 56 + mc:57 + mc], ALU.add),
                        reads=[cols_b], writes=[r1_b[mc]], inc=cv)
            th.append(x2load)

            def wout_unit(mc):
                def f():
                    wr = wor_b[mc % 2]
                    p.emit("sync", lambda s: s.dma_start(
                        out=wr.ap.rearrange("p (c j) -> p c j", c=DC),
                        in_=wout_scr[mc]), writes=[wr], inc=d_s)
                    for kc in range(DC):
                        for hh in range(2):
                            last = (kc == DC - 1 and hh == 1)
                            p.emit("tensor", lambda t, kc=kc, hh=hh: t.matmul(
                                psD[:, hh * 512:(hh + 1) * 512],
                                wr.ap[:, kc * 128:(kc + 1) * 128],
                                yt[par][:, kc * TC + hh * 512:kc * TC + (hh + 1) * 512],
                                start=(kc == 0), stop=(kc == DC - 1)),
                                reads=[wr] + (yt_b[par] if last else [yt_b[par][kc]]),
                                writes=[psD_re, psD_im] if last else (),
                                war=() if (kc or hh) else [psD_re, psD_im],
                                inc=ct if last else None)
                    p.emit("vector", lambda v: v.scalar_tensor_tensor(
                        r1t[:, mc * TC:(mc + 1) * TC], psD[:, :],
                        cols[:, 16 + mc:17 + mc],
                        r1t[:, mc * TC:(mc + 1) * TC], ALU.mult, ALU.add),
                        reads=[psD_re, psD_im, cols_b], writes=[r1_b[mc]], inc=cv)
                return f
            for mc in range(DC):
                th.append(wout_unit(mc))

            th += ln_stats_rows_bcast(r1_b, 1, ab1, ab1_b, with_s=True)

            def h2_unit(dc):
                def f():
                    p.emit("vector", lambda v: v.tensor_mul(
                        h2t[:, dc * TC:(dc + 1) * TC],
                        r1t[:, dc * TC:(dc + 1) * TC], ab1[:, 0:TC]),
                        reads=[r1_b[dc], ab1_b], writes=[h2_b[dc]], inc=cv)
                    p.emit("vector", lambda v: v.tensor_add(
                        h2t[:, dc * TC:(dc + 1) * TC],
                        h2t[:, dc * TC:(dc + 1) * TC], ab1[:, TC:2 * TC]),
                        reads=[ab1_b], writes=[h2_b[dc]], inc=cv)
                    p.emit("vector", lambda v: v.tensor_scalar(
                        h2t[:, dc * TC:(dc + 1) * TC],
                        h2t[:, dc * TC:(dc + 1) * TC],
                        cols[:, 32 + dc:33 + dc],
                        cols[:, 24 + dc:25 + dc], ALU.mult, ALU.add),
                        reads=[cols_b], writes=[h2_b[dc]], inc=cv)
                return f
            for dc in range(DC):
                th.append(h2_unit(dc))

            def ffn_unit(mc):
                def f():
                    wc_ = wcr_b[mc % 2]
                    ws_ = wsr_b[mc % 2]
                    p.emit("sync", lambda s: s.dma_start(
                        out=wc_.ap.rearrange("p (c j) -> p c j", c=DC),
                        in_=wc_scr[mc]), writes=[wc_], inc=d_s)
                    p.emit("sync", lambda s: s.dma_start(
                        out=ws_.ap.rearrange("p (c j) -> p c j", c=DC),
                        in_=ws_scr[mc]), writes=[ws_], inc=d_s)
                    for hh in range(2):
                        p.emit("tensor", lambda t, hh=hh: t.matmul(
                            psD[:, 0:512], ones_rf[0:1, :],
                            rowf[0:1, 2 * TC + hh * 512:2 * TC + (hh + 1) * 512],
                            start=True, stop=False),
                            reads=[rowf_b, onesrf_b], war=[psD_re], inc=None)
                        for kc in range(DC):
                            last = (kc == DC - 1)
                            p.emit("tensor", lambda t, kc=kc, hh=hh: t.matmul(
                                psD[:, 0:512],
                                wc_.ap[:, kc * 128:(kc + 1) * 128],
                                h2t[:, kc * TC + hh * 512:kc * TC + (hh + 1) * 512],
                                start=False, stop=last),
                                reads=[wc_] + (h2_b if last else [h2_b[kc]]),
                                writes=[psD_re] if last else (),
                                inc=ct if last else None)
                        for kc in range(DC):
                            last = (kc == DC - 1)
                            p.emit("tensor", lambda t, kc=kc, hh=hh: t.matmul(
                                psD[:, 512:1024],
                                ws_.ap[:, kc * 128:(kc + 1) * 128],
                                h2t[:, kc * TC + hh * 512:kc * TC + (hh + 1) * 512],
                                start=(kc == 0), stop=last),
                                reads=[ws_] + (h2_b if last else [h2_b[kc]]),
                                writes=[psD_im] if last else (),
                                war=() if kc else [psD_im],
                                inc=ct if last else None)
                        p.emit("scalar", lambda s: s.activation(
                            psD[:, 0:512], psD[:, 0:512], AF.Square),
                            reads=[psD_re], writes=[psD_re], inc=cs)
                        p.emit("scalar", lambda s, hh=hh: s.activation(
                            outs[:, hh * 512:(hh + 1) * 512], psD[:, 512:1024],
                            AF.Square),
                            reads=[psD_im], writes=[outs_b[hh]], inc=cs)
                        p.emit("vector", lambda v, hh=hh: v.tensor_add(
                            outs[:, hh * 512:(hh + 1) * 512],
                            outs[:, hh * 512:(hh + 1) * 512], psD[:, 0:512]),
                            reads=[outs_b[hh], psD_re], writes=[outs_b[hh]], inc=cv)
                        p.emit("scalar", lambda s, hh=hh: s.activation(
                            magb[:], outs[:, hh * 512:(hh + 1) * 512], AF.Sqrt),
                            reads=[outs_b[hh]], writes=[magb_b[0]], inc=cs)
                        p.emit("vector", lambda v, hh=hh: v.scalar_tensor_tensor(
                            outs[:, hh * 512:(hh + 1) * 512],
                            magb[:],
                            cols[:, 40 + mc:41 + mc],
                            r1t[:, mc * TC + hh * 512:mc * TC + (hh + 1) * 512],
                            ALU.mult, ALU.add),
                            reads=[magb_b[0], cols_b, r1_b[mc]],
                            writes=[outs_b[hh]], inc=cv)
                        p.emit("sync", lambda s, hh=hh: s.dma_start(
                            out=outT[mc * 128:(mc + 1) * 128,
                                     t0 + hh * 512:t0 + (hh + 1) * 512],
                            in_=outs[:, hh * 512:(hh + 1) * 512]),
                            reads=[outs_b[hh]], inc=d_o)
                return f
            for mc in range(DC):
                th.append(ffn_unit(mc))
            return th

        def stage_C(tc, bg):
            par = tc % 2
            bgi = 0
            for n in range(N_ST):
                npar = n % 2
                for (half, dst) in ((0, Bb_b[npar]), (1, Cb_b[npar])):
                    for hh in range(2):
                        last = (hh == 1)
                        p.emit("tensor", lambda t, half=half, hh=hh, n=n: t.matmul(
                            psC[:, hh * 512:(hh + 1) * 512],
                            selB[:, n * 128:(n + 1) * 128],
                            bc_sb[:, par * 2 * TC + half * TC + hh * 512:
                                  par * 2 * TC + half * TC + (hh + 1) * 512],
                            start=True, stop=True),
                            reads=[selB_b, bc_b[par]],
                            writes=[psC_b] if last else (),
                            war=() if hh else [psC_b],
                            inc=ct if last else None)
                    p.emit("scalar", lambda s, dst=dst: s.activation(
                        dst.ap, psC[:, :], AF.Copy),
                        reads=[psC_b], writes=[dst], inc=cs)
                for dc in range(DC):
                    it = n * DC + dc
                    slot = it % 2
                    i_st = dc * N_ST + n
                    ab_, bx_, h_, z_ = abr_b[slot], bxr_b[slot], hr_b[slot], zr_b[slot]
                    p.emit("scalar", lambda s, ab_=ab_, dc=dc, i_st=i_st: s.activation(
                        ab_.ap, dl[par][:, dc * TC:(dc + 1) * TC],
                        AF.Exp, scale=a_sc[:, i_st:i_st + 1]),
                        reads=[dl_b[par][dc], asc_b], writes=[ab_], inc=cs)
                    p.emit("vector", lambda v, bx_=bx_, dc=dc, npar=npar:
                           v.tensor_mul(bx_.ap,
                                        xch[par][:, dc * TC:(dc + 1) * TC],
                                        Bbt[:, npar * TC:(npar + 1) * TC]),
                           reads=[ch_b[par][dc], Bb_b[npar]], writes=[bx_], inc=cv)
                    p.emit("scalar", lambda s, ab_=ab_, i_st=i_st, slot=slot:
                           s.activation(
                        ctmp[:, slot:slot + 1], ab_.ap[:, 0:1], AF.Copy,
                        scale=stt[:, i_st:i_st + 1]),
                        reads=[ab_, stt_b[i_st]], writes=[ctmp_b[slot]], inc=cs)
                    p.emit("vector", lambda v, bx_=bx_, slot=slot: v.tensor_add(
                        bx_.ap[:, 0:1], bx_.ap[:, 0:1], ctmp[:, slot:slot + 1]),
                        reads=[ctmp_b[slot]], writes=[bx_], inc=cv)
                    p.emit("vector", lambda v, h_=h_, ab_=ab_, bx_=bx_:
                           v.tensor_tensor_scan(h_.ap, ab_.ap, bx_.ap, 0.0,
                                                ALU.mult, ALU.add),
                           reads=[ab_, bx_], writes=[h_], inc=cv)
                    p.emit("gpsimd", lambda g, h_=h_, i_st=i_st: g.tensor_copy(
                        stt[:, i_st:i_st + 1], h_.ap[:, TC - 1:TC]),
                        reads=[h_], writes=[stt_b[i_st]], inc=cp)
                    z_eng = "gpsimd" if (it % 4 == 1) else "vector"
                    z_chan = cp if z_eng == "gpsimd" else cv
                    p.emit(z_eng, lambda e, z_=z_, h_=h_, npar=npar: e.tensor_mul(
                        z_.ap, h_.ap,
                        Cbt[:, npar * TC:(npar + 1) * TC]),
                        reads=[h_, Cb_b[npar]], writes=[z_], inc=z_chan)
                    p.emit("gpsimd", lambda g, z_=z_, dc=dc: g.tensor_add(
                        yt[par][:, dc * TC:(dc + 1) * TC],
                        yt[par][:, dc * TC:(dc + 1) * TC], z_.ap),
                        reads=[z_], writes=[yt_b[par][dc]], inc=cp)
                n_take = (len(bg) * (n + 1)) // N_ST - (len(bg) * n) // N_ST
                for _ in range(n_take):
                    if bgi < len(bg):
                        bg[bgi]()
                        bgi += 1
            while bgi < len(bg):
                bg[bgi]()
                bgi += 1

        for f in stage_A(0):
            f()
        for f in stage_B(0):
            f()
        for tc in range(NTC):
            bg = []
            if tc > 0:
                bg += stage_D(tc - 1)
            if tc + 1 < NTC:
                bg += stage_A(tc + 1)
                bg += stage_B(tc + 1)
            stage_C(tc, bg)
        for f in stage_D(NTC - 1):
            f()

        p.emit("sync", None, extra_waits=[d_o])
        p.build(block)
    return nc


_NC_CACHE = {}


def kernel(L0_attn_x, L0_attn_A_log, L0_attn_Dp, L0_attn_W_xproj, L0_attn_b_xproj,
           L0_attn_W_out, L0_attn_b_out, L0_ffn_theta, L0_ffn_phi,
           L0_ffn_theta_noise, L0_ffn_phi_noise, L0_c, L0_wmod):
    B = L0_attn_x.shape[0]
    f32 = np.float32
    cc = np.ascontiguousarray

    A_log_r = cc(np.asarray(L0_attn_A_log, f32).reshape(DC, 128, N_ST)
                 .transpose(1, 0, 2).reshape(128, DC * N_ST))
    Dp_c = cc(np.asarray(L0_attn_Dp, f32).reshape(DC, 128).T)
    bxp_c = cc(np.asarray(L0_attn_b_xproj[:D], f32).reshape(DC, 128).T)
    _bbc = np.asarray(L0_attn_b_xproj[D:], f32)
    bbc_c = cc(np.stack([_bbc[:N_ST], _bbc[N_ST:]], axis=1))
    bout_c = cc(np.asarray(L0_attn_b_out, f32).reshape(DC, 128).T)

    if "nc" not in _NC_CACHE:
        _NC_CACHE["nc"] = build_program()
    nc = _NC_CACHE["nc"]

    in_maps = []
    for b in range(B):
        in_maps.append(dict(
            xT=cc(np.asarray(L0_attn_x[b], f32).T),
            A_log_r=A_log_r,
            Dp_c=Dp_c,
            Wxp=cc(np.asarray(L0_attn_W_xproj, f32)),
            bxp_c=bxp_c,
            bout_row=cc(np.asarray(L0_attn_b_out, f32).reshape(1, D)),
            bbc_c=bbc_c,
            Wout=cc(np.asarray(L0_attn_W_out, f32)),
            bout_c=bout_c,
            theta=cc(np.asarray(L0_ffn_theta, f32)),
            phi=cc(np.asarray(L0_ffn_phi, f32)),
            theta_n=cc(np.asarray(L0_ffn_theta_noise, f32)),
            phi_n=cc(np.asarray(L0_ffn_phi_noise, f32)),
            c_col=cc(np.asarray(L0_c[b], f32).reshape(DC, 128).T),
            wmod=cc(np.asarray(L0_wmod, f32)),
        ))

    res = run_bass_kernel_spmd(nc, in_maps, core_ids=list(range(B)))
    out = np.empty((B, L, D), np.float32)
    for b in range(B):
        out[b] = res.results[b]["outT"].T
    return out


# version 20
# speedup vs baseline: 1.1994x; 1.1990x over previous
import numpy as np
import ml_dtypes
from contextlib import ExitStack

import concourse.bass as bass
import concourse.mybir as mybir
from concourse.bass_utils import run_bass_kernel_spmd

F32 = mybir.dt.float32
BF16 = mybir.dt.bfloat16
AF = mybir.ActivationFunctionType
ALU = mybir.AluOpType

ENGINES = ("sync", "scalar", "vector", "tensor", "gpsimd")

D = 1024
L = 4096
N_ST = 16
DC = 8
TB = 1024
NTB = L // TB
EPS = 1e-5



class Chan:

    def __init__(self, handle, dma=False):
        self.h = handle
        self.dma = dma
        self.count = 0
        self.amt = 16 if dma else 1


class Buf:

    def __init__(self, ap, name=""):
        self.ap = ap
        self.name = name
        self.last_write = None
        self.readers = []


class Prog:
    def __init__(self, nc):
        self.nc = nc
        self.steps = {e: [] for e in ENGINES}
        self.hwm = {e: {} for e in ENGINES}
        self.all_chans = []

    def chan(self, es, name, dma=False):
        h = es.enter_context(self.nc.semaphore(name))
        c = Chan(h, dma=dma)
        self.all_chans.append(c)
        return c

    def emit(self, engine, fn, reads=(), writes=(), extra_waits=(), inc=None,
             war=()):
        need = {}

        def add(cc):
            if cc is None:
                return
            ch, v = cc
            if v <= 0:
                return
            if need.get(ch, 0) < v:
                need[ch] = v

        for b in reads:
            add(b.last_write)
        for b in list(writes) + list(war):
            add(b.last_write)
            for r in b.readers:
                add(r)
        for w in extra_waits:
            add(w if isinstance(w, tuple) else (w, w.count))

        if inc is not None and inc.dma:
            add((inc, inc.count))
        hw = self.hwm[engine]
        waits = []
        for ch, v in need.items():
            if hw.get(ch, 0) < v:
                waits.append((ch, v))
                hw[ch] = v
        if inc is not None:
            inc.count += inc.amt
        self.steps[engine].append((fn, waits, inc))
        tag = (inc, inc.count) if inc is not None else None
        for b in writes:
            b.last_write = tag
            b.readers = []
        if tag is not None:
            for b in reads:
                b.readers.append(tag)

    def barrier(self):
        for e in ENGINES:
            for ch in self.all_chans:
                if ch.count > 0 and self.hwm[e].get(ch, 0) < ch.count:
                    self.steps[e].append((None, [(ch, ch.count)], None))
                    self.hwm[e][ch] = ch.count

    def build(self, block):
        def mk(engine):
            def run(eng):
                for fn, waits, inc in self.steps[engine]:
                    for ch, v in waits:
                        eng.wait_ge(ch.h, v)
                    if fn is not None:
                        ins = fn(eng)
                        if inc is not None:
                            assert ins is not None
                            ins.then_inc(inc.h, inc.amt)
                    elif inc is not None:
                        eng.sem_inc(inc.h, inc.amt)
            return run
        block.sync(mk("sync"))
        block.scalar(mk("scalar"))
        block.vector(mk("vector"))
        block.tensor(mk("tensor"))
        block.gpsimd(mk("gpsimd"))



def build_program():
    nc = bass.Bass("TRN2", target_bir_lowering=False)

    xT = nc.dram_tensor("xT", [D, L], F32, kind="ExternalInput").ap()
    A_log_r = nc.dram_tensor("A_log_r", [128, DC * N_ST], F32, kind="ExternalInput").ap()
    Dp_c = nc.dram_tensor("Dp_c", [128, DC], F32, kind="ExternalInput").ap()
    Wxp = nc.dram_tensor("Wxp", [D, D + 2 * N_ST], F32, kind="ExternalInput").ap()
    bxp_c = nc.dram_tensor("bxp_c", [128, DC], F32, kind="ExternalInput").ap()
    bbc_c = nc.dram_tensor("bbc_c", [2 * N_ST, 1], F32, kind="ExternalInput").ap()
    Wout = nc.dram_tensor("Wout", [D, D], F32, kind="ExternalInput").ap()
    bout_c = nc.dram_tensor("bout_c", [128, DC], F32, kind="ExternalInput").ap()
    theta = nc.dram_tensor("theta", [D, D], F32, kind="ExternalInput").ap()
    phi = nc.dram_tensor("phi", [D, D], F32, kind="ExternalInput").ap()
    theta_n = nc.dram_tensor("theta_n", [D, D], F32, kind="ExternalInput").ap()
    phi_n = nc.dram_tensor("phi_n", [D, D], F32, kind="ExternalInput").ap()
    c_col = nc.dram_tensor("c_col", [128, DC], F32, kind="ExternalInput").ap()
    wmod = nc.dram_tensor("wmod", [D, 6 * D], F32, kind="ExternalInput").ap()

    outT = nc.dram_tensor("outT", [D, L], F32, kind="ExternalOutput").ap()

    h1_scr = nc.dram_tensor("h1_scr", [DC, 128, L], BF16).ap()
    du_scr = nc.dram_tensor("du_scr", [DC, 128, 2 * L], BF16).ap()
    wo_scr = nc.dram_tensor("wo_scr", [D, D], BF16).ap()
    wc_scr = nc.dram_tensor("wc_scr", [D, D], BF16).ap()
    ws_scr = nc.dram_tensor("ws_scr", [D, D], BF16).ap()

    es = ExitStack()
    with es:
        p = Prog(nc)

        def sb(name, shape, dt):
            return es.enter_context(nc.sbuf_tensor(name, shape, dt))

        big1 = sb("big1", [128, 8 * L], BF16)
        big2 = sb("big2", [128, 8 * L], BF16)
        big4 = sb("big4", [128, 3 * L], BF16)
        wor = sb("wor", [128, 2 * DC * 128], BF16)
        wcr = sb("wcr", [128, 2 * DC * 128], BF16)
        wsr = sb("wsr", [128, 2 * DC * 128], BF16)
        bc_sb = sb("bc_sb", [2 * N_ST, L], BF16)
        bcstg = sb("bcstg", [1, L], BF16)
        cons = sb("cons", [128, 64], F32)
        a_sc = sb("a_sc", [128, 128], F32)
        mc_adj = sb("mc_adj", [128, 32], BF16)
        rows1 = sb("rows1", [2, D], BF16)
        rows2 = sb("rows2", [2, D], BF16)
        ab_a = sb("ab_a", [1, TB], BF16)
        ab_bo = sb("ab_bo", [2, TB], BF16)
        rowf = sb("rowf", [1, 3 * TB], F32)
        ones1 = sb("ones1", [128, 1], BF16)
        ones_r = sb("ones_r", [1, 128], BF16)
        sio = sb("sio", [128, DC], F32)
        siob = sb("siob", [128, DC], BF16)
        gcols = sb("gcols", [128, 4 * DC], F32)
        mcols_f = sb("mcols_f", [128, 48], F32)

        ps_all = es.enter_context(nc.psum_tensor("ps_all", [128, 4 * TB], F32))

        d_g = p.chan(es, "d_g", dma=True)
        d_s = p.chan(es, "d_s", dma=True)
        d_w = p.chan(es, "d_w", dma=True)
        d_o = p.chan(es, "d_o", dma=True)
        cv = p.chan(es, "cv")
        cs = p.chan(es, "cs")
        ct = p.chan(es, "ct")
        cg = p.chan(es, "cg")

        block = es.enter_context(nc.Block())

        asc_b = Buf(a_sc[:], "a_sc")
        p.emit("sync", lambda s: s.dma_start(out=a_sc[:], in_=A_log_r),
               writes=[asc_b], inc=d_s)
        p.emit("scalar", lambda s: s.activation(a_sc[:], a_sc[:], AF.Exp),
               reads=[asc_b], writes=[asc_b], inc=cs)
        p.emit("vector", lambda v: v.tensor_scalar_mul(a_sc[:], a_sc[:], -1.0),
               reads=[asc_b], writes=[asc_b], inc=cv)

        dp_b = Buf(cons[:, 0:DC], "dp")
        bxpc_b = Buf(cons[:, 8:8 + DC], "bxp")
        boutc_b = Buf(cons[:, 16:16 + DC], "bout")
        bbc_b = Buf(cons[0:2 * N_ST, 24:25], "bbc")
        p.emit("sync", lambda s: s.dma_start(out=cons[:, 0:DC], in_=Dp_c),
               writes=[dp_b], inc=d_s)
        p.emit("sync", lambda s: s.dma_start(out=cons[:, 8:8 + DC], in_=bxp_c),
               writes=[bxpc_b], inc=d_s)
        p.emit("sync", lambda s: s.dma_start(out=cons[:, 16:16 + DC], in_=bout_c),
               writes=[boutc_b], inc=d_s)
        p.emit("sync", lambda s: s.dma_start(out=cons[0:2 * N_ST, 24:25], in_=bbc_c),
               writes=[bbc_b], inc=d_s)

        sio_b = Buf(sio[:], "sio")
        siob_b = Buf(siob[:], "siob")
        p.emit("sync", lambda s: s.dma_start(out=sio[:], in_=c_col),
               writes=[sio_b], inc=d_s)
        p.emit("scalar", lambda s: s.activation(sio[:], sio[:], AF.Silu),
               reads=[sio_b], writes=[sio_b], inc=cs)
        p.emit("vector", lambda v: v.tensor_copy(siob[:], sio[:]),
               reads=[sio_b], writes=[siob_b], inc=cv)

        stg_a = Buf(big2[:, 0:8 * D], "stg_a")
        stg_b = Buf(big2[:, 8 * D:16 * D], "stg_b")
        stg_c = Buf(big2[:, 16 * D:24 * D], "stg_c")
        stg_d = Buf(big2[:, 24 * D:32 * D], "stg_d")

        def cast_weight_to_scr(w_in, w_out, stg):
            src = w_in.rearrange("(c p) j -> p c j", p=128)
            dst = w_out.rearrange("(c p) j -> p c j", p=128)
            sview = stg.ap.rearrange("p (c j) -> p c j", c=DC)
            p.emit("gpsimd", lambda g: g.dma_start(out=sview, in_=src),
                   writes=[stg], inc=d_g)
            p.emit("sync", lambda s: s.dma_start(out=dst, in_=sview),
                   reads=[stg], inc=d_w)

        cast_weight_to_scr(Wout, wo_scr, stg_a)

        def build_trig(w1, w2, w_out, stg1, stg2, is_cos):
            src1 = w1.rearrange("(c p) j -> p c j", p=128)
            src2 = w2.rearrange("(c p) j -> p c j", p=128)
            s1 = stg1.ap
            s2 = stg2.ap
            v1 = s1.rearrange("p (c j) -> p c j", c=DC)
            v2 = s2.rearrange("p (c j) -> p c j", c=DC)
            p.emit("gpsimd", lambda g: g.dma_start(out=v1, in_=src1),
                   writes=[stg1], inc=d_g)
            p.emit("gpsimd", lambda g: g.dma_start(out=v2, in_=src2),
                   writes=[stg2], inc=d_g)
            p.emit("vector", lambda v: v.tensor_add(s1, s1, s2),
                   reads=[stg1, stg2], writes=[stg1], inc=cv)
            p.emit("vector", lambda v: v.tensor_mul(s2, s1, s1),
                   reads=[stg1], writes=[stg2], inc=cv)
            if is_cos:
                p.emit("vector", lambda v: v.tensor_scalar(s2, s2, 1.0 / 24.0, -0.5,
                                                           ALU.mult, ALU.add),
                       reads=[stg2], writes=[stg2], inc=cv)
                p.emit("vector", lambda v: v.tensor_mul(s2, s2, s1),
                       reads=[stg1, stg2], writes=[stg2], inc=cv)
                p.emit("vector", lambda v: v.tensor_mul(s2, s2, s1),
                       reads=[stg1, stg2], writes=[stg2], inc=cv)
                p.emit("vector", lambda v: v.tensor_scalar_add(s1, s2, 1.0),
                       reads=[stg2], writes=[stg1], inc=cv)
            dst = w_out.rearrange("(c p) j -> p c j", p=128)
            sview = s1.rearrange("p (c j) -> p c j", c=DC)
            p.emit("sync", lambda s: s.dma_start(out=dst, in_=sview),
                   reads=[stg1], inc=d_w)

        build_trig(theta, theta_n, wc_scr, stg_a, stg_b, True)

        srcp = phi.rearrange("(c p) j -> p c j", p=128)
        srcpn = phi_n.rearrange("(c p) j -> p c j", p=128)
        sA, sB = stg_c, stg_d
        vA = sA.ap.rearrange("p (c j) -> p c j", c=DC)
        vB = sB.ap.rearrange("p (c j) -> p c j", c=DC)
        p.emit("gpsimd", lambda g: g.dma_start(out=vA, in_=srcp), writes=[sA], inc=d_g)
        p.emit("gpsimd", lambda g: g.dma_start(out=vB, in_=srcpn), writes=[sB], inc=d_g)
        p.emit("vector", lambda v: v.tensor_add(sA.ap, sA.ap, sB.ap),
               reads=[sA, sB], writes=[sA], inc=cv)
        p.emit("vector", lambda v: v.tensor_mul(sB.ap, sA.ap, sA.ap),
               reads=[sA], writes=[sB], inc=cv)
        p.emit("vector", lambda v: v.tensor_scalar(sB.ap, sB.ap, 1.0 / 120.0,
                                                   -1.0 / 6.0, ALU.mult, ALU.add),
               reads=[sB], writes=[sB], inc=cv)
        p.emit("vector", lambda v: v.tensor_mul(sB.ap, sB.ap, sA.ap),
               reads=[sA, sB], writes=[sB], inc=cv)
        p.emit("vector", lambda v: v.tensor_mul(sB.ap, sB.ap, sA.ap),
               reads=[sA, sB], writes=[sB], inc=cv)
        p.emit("vector", lambda v: v.tensor_mul(sB.ap, sB.ap, sA.ap),
               reads=[sA, sB], writes=[sB], inc=cv)
        p.emit("vector", lambda v: v.tensor_add(sB.ap, sB.ap, sA.ap),
               reads=[sA, sB], writes=[sB], inc=cv)
        dstws = ws_scr.rearrange("(c p) j -> p c j", p=128)
        p.emit("sync", lambda s: s.dma_start(out=dstws, in_=vB), reads=[sB], inc=d_w)

        wm1 = Buf(big1[:, 0:6 * D], "wm1")
        wm2 = Buf(big1[:, 6 * D:12 * D], "wm2")
        mcol_ps = Buf(ps_all[:, 0:48], "mcol_ps")
        mcols_b = Buf(mcols_f[:], "mcols")
        p.emit("vector", lambda v: v.memset(mcols_f[:], 0.0),
               writes=[mcols_b], inc=cv)
        for kc in range(DC):
            wmb = (wm1, wm2)[kc % 2]
            src = wmod[kc * 128:(kc + 1) * 128, :]
            p.emit("gpsimd", lambda g, wmb=wmb, src=src:
                   g.dma_start(out=wmb.ap, in_=src), writes=[wmb], inc=d_g)
            for j in range(48):
                grp, dcj = j // DC, j % DC
                last = (j == 47)
                p.emit("tensor", lambda t, wmb=wmb, kc=kc, j=j, grp=grp, dcj=dcj:
                       t.matmul(mcol_ps.ap[:, j:j + 1],
                                wmb.ap[:, grp * D + dcj * 128:grp * D + (dcj + 1) * 128],
                                siob[:, kc:kc + 1],
                                start=True, stop=True),
                       reads=[wmb, siob_b], writes=[mcol_ps] if last else (),
                       war=() if j else [mcol_ps],
                       inc=ct if last else None)
            p.emit("vector", lambda v: v.tensor_add(mcols_f[:], mcols_f[:],
                                                    mcol_ps.ap),
                   reads=[mcol_ps], writes=[mcols_b], inc=cv)

        mc_b = Buf(mc_adj[:], "mc_adj")
        p.emit("vector", lambda v: v.tensor_scalar_add(
            mc_adj[:, 0:16:2], mcols_f[:, 8:16], 1.0),
            reads=[mcols_b], writes=[mc_b], inc=cv)
        p.emit("vector", lambda v: v.tensor_copy(mc_adj[:, 1:16:2], mcols_f[:, 0:8]),
               reads=[mcols_b], writes=[mc_b], inc=cv)
        p.emit("vector", lambda v: v.tensor_scalar_add(
            mc_adj[:, 16:32:2], mcols_f[:, 32:40], 1.0),
            reads=[mcols_b], writes=[mc_b], inc=cv)
        p.emit("vector", lambda v: v.tensor_copy(mc_adj[:, 17:32:2], mcols_f[:, 24:32]),
               reads=[mcols_b], writes=[mc_b], inc=cv)

        rows1_b = Buf(rows1[:], "rows1")
        rows2_b = Buf(rows2[:], "rows2")
        for ln_i, (rt, rb_) in enumerate(((rows1, rows1_b), (rows2, rows2_b))):
            for dc in range(DC):
                for r in range(2):
                    col = ln_i * 16 + 2 * dc + r
                    p.emit("sync", lambda s, rt=rt, r=r, dc=dc, col=col: s.dma_start(
                        out=rt[r:r + 1, dc * 128:(dc + 1) * 128],
                        in_=mc_adj[:, col:col + 1],
                        allow_slow_non_contiguous=True),
                        reads=[mc_b], writes=[rb_], inc=d_s)

        g_b = Buf(gcols[:], "gcols")
        for gi, base in enumerate((16, 40)):
            dst = gcols[:, gi * DC:(gi + 1) * DC]
            tmp = gcols[:, (2 + gi) * DC:(3 + gi) * DC]
            src = mcols_f[:, base:base + DC]
            p.emit("vector", lambda v, tmp=tmp, src=src:
                   v.tensor_scalar(tmp, src, 0.0, 2.0, ALU.is_ge, ALU.mult),
                   reads=[mcols_b], writes=[g_b], inc=cv)
            p.emit("vector", lambda v, tmp=tmp:
                   v.tensor_scalar_add(tmp, tmp, -1.0),
                   reads=[g_b], writes=[g_b], inc=cv)
            p.emit("scalar", lambda s, dst=dst, src=src:
                   s.activation(dst, src, AF.Abs),
                   reads=[mcols_b], writes=[g_b], inc=cs)
            p.emit("vector", lambda v, dst=dst:
                   v.tensor_scalar_max(dst, dst, 0.25),
                   reads=[g_b], writes=[g_b], inc=cv)
            p.emit("vector", lambda v, dst=dst, tmp=tmp:
                   v.tensor_mul(dst, dst, tmp),
                   reads=[g_b], writes=[g_b], inc=cv)

        ones_b = Buf(ones1[:], "ones1")
        p.emit("vector", lambda v: v.memset(ones1[:], 1.0), writes=[ones_b], inc=cv)
        onesr_b = Buf(ones_r[:], "ones_r")
        p.emit("vector", lambda v: v.memset(ones_r[:], 1.0), writes=[onesr_b], inc=cv)
        aba_b = Buf(ab_a[:], "ab_a")
        abbo_b = Buf(ab_bo[:], "ab_bo")
        p.emit("vector", lambda v: v.memset(ab_bo[0:2, :], 1.0),
               writes=[abbo_b], inc=cv)

        p.barrier()
        wx_b = Buf(big1[:, 0:DC * 1056], "wx")
        wxv = big1[:, 0:DC * 1056].rearrange("p (c j) -> p c j", c=DC)
        p.emit("gpsimd", lambda g: g.dma_start(
            out=wxv, in_=Wxp.rearrange("(c p) j -> p c j", p=128)),
            writes=[wx_b], inc=d_g)

        WX_END = DC * 1056
        xs_p = [Buf(big2[:, 0:8 * TB], "x_bf0"),
                Buf(big1[:, WX_END:WX_END + 8 * TB], "x_bf1")]
        h1_p = [Buf(big2[:, 8 * TB:16 * TB], "h1_0"),
                Buf(big1[:, WX_END + 8 * TB:WX_END + 16 * TB], "h1_1")]
        dls = Buf(big2[:, 16 * TB:24 * TB], "dl")
        uss = Buf(big2[:, 24 * TB:32 * TB], "u")
        xsq = Buf(big4[:, 0:8 * TB], "xsq")
        esb = Buf(big4[:, 8 * TB:9 * TB], "esb")
        r_ps0 = Buf(ps_all[:, 0:TB], "ps0")
        r_ps1 = Buf(ps_all[:, TB:2 * TB], "ps1")
        r_ps2 = Buf(ps_all[:, 2 * TB:3 * TB], "ps2")
        r_ps3 = Buf(ps_all[:, 3 * TB:4 * TB], "ps3")
        rowf_b = Buf(rowf[:], "rowf")
        bc_b = Buf(bc_sb[:], "bc")

        def ln_stats_and_coeffs(src_b):
            p.emit("vector", lambda v: v.tensor_mul(xsq.ap, src_b.ap, src_b.ap),
                   reads=[src_b], writes=[xsq], inc=cv)
            for dc in range(DC):
                for hh in range(2):
                    last = (dc == DC - 1 and hh == 1)
                    p.emit("tensor", lambda t, dc=dc, hh=hh: t.matmul(
                        r_ps0.ap[0:1, hh * 512:(hh + 1) * 512], ones1[:, 0:1],
                        src_b.ap[:, dc * TB + hh * 512:dc * TB + (hh + 1) * 512],
                        start=(dc == 0), stop=(dc == DC - 1)),
                        reads=[src_b, ones_b], writes=[r_ps0] if last else (),
                        war=() if (dc or hh) else [r_ps0],
                        inc=ct if last else None)
            for dc in range(DC):
                for hh in range(2):
                    last = (dc == DC - 1 and hh == 1)
                    p.emit("tensor", lambda t, dc=dc, hh=hh: t.matmul(
                        r_ps1.ap[0:1, hh * 512:(hh + 1) * 512], ones1[:, 0:1],
                        xsq.ap[:, dc * TB + hh * 512:dc * TB + (hh + 1) * 512],
                        start=(dc == 0), stop=(dc == DC - 1)),
                        reads=[xsq, ones_b], writes=[r_ps1] if last else (),
                        war=() if (dc or hh) else [r_ps1],
                        inc=ct if last else None)
            p.emit("vector", lambda v: v.tensor_scalar_mul(rowf[0:1, 0:TB],
                                                           r_ps0.ap[0:1, :], 1.0 / D),
                   reads=[r_ps0], writes=[rowf_b], inc=cv)
            p.emit("scalar", lambda s: s.activation(rowf[0:1, TB:2 * TB], rowf[0:1, 0:TB],
                                                    AF.Square),
                   reads=[rowf_b], writes=[rowf_b], inc=cs)
            p.emit("vector", lambda v: v.tensor_scalar(rowf[0:1, 2 * TB:3 * TB],
                                                       r_ps1.ap[0:1, :], 1.0 / D,
                                                       EPS, ALU.mult, ALU.add),
                   reads=[r_ps1], writes=[rowf_b], inc=cv)
            p.emit("vector", lambda v: v.tensor_sub(rowf[0:1, 2 * TB:3 * TB], rowf[0:1, 2 * TB:3 * TB],
                                                    rowf[0:1, TB:2 * TB]),
                   reads=[rowf_b], writes=[rowf_b], inc=cv)
            p.emit("scalar", lambda s: s.activation(rowf[0:1, 2 * TB:3 * TB], rowf[0:1, 2 * TB:3 * TB],
                                                    AF.Ln),
                   reads=[rowf_b], writes=[rowf_b], inc=cs)
            p.emit("vector", lambda v: v.tensor_scalar_mul(rowf[0:1, 2 * TB:3 * TB],
                                                           rowf[0:1, 2 * TB:3 * TB], -0.5),
                   reads=[rowf_b], writes=[rowf_b], inc=cv)
            p.emit("scalar", lambda s: s.activation(rowf[0:1, 2 * TB:3 * TB], rowf[0:1, 2 * TB:3 * TB],
                                                    AF.Exp),
                   reads=[rowf_b], writes=[rowf_b], inc=cs)
            p.emit("vector", lambda v: v.tensor_copy(ab_a[0:1, :], rowf[0:1, 2 * TB:3 * TB]),
                   reads=[rowf_b], writes=[aba_b], inc=cv)
            p.emit("vector", lambda v: v.scalar_tensor_tensor(
                ab_bo[0:1, :], rowf[0:1, 0:TB], -1.0, rowf[0:1, 2 * TB:3 * TB],
                ALU.mult, ALU.mult),
                reads=[rowf_b], writes=[abbo_b], inc=cv)

        def ln_apply(dst_b, src_b, rows_t, rows_tb):
            for dc in range(DC):
                sl128 = slice(dc * 128, (dc + 1) * 128)
                for hh in range(2):
                    h5 = slice(hh * 512, (hh + 1) * 512)
                    p.emit("tensor", lambda t, sl128=sl128, rows_t=rows_t, h5=h5:
                           t.matmul(r_ps0.ap[:, h5], rows_t[0:1, sl128],
                                    ab_a[0:1, h5], start=True, stop=True),
                           reads=[rows_tb, aba_b], writes=[r_ps0], inc=ct)
                    p.emit("tensor", lambda t, sl128=sl128, rows_t=rows_t, h5=h5:
                           t.matmul(r_ps1.ap[:, h5], rows_t[0:2, sl128],
                                    ab_bo[0:2, h5], start=True, stop=True),
                           reads=[rows_tb, abbo_b], writes=[r_ps1], inc=ct)
                sl = slice(dc * TB, (dc + 1) * TB)
                p.emit("vector", lambda v, sl=sl: v.tensor_mul(
                    dst_b.ap[:, sl], src_b.ap[:, sl], r_ps0.ap[:, :]),
                    reads=[src_b, r_ps0], writes=[dst_b], inc=cv)
                p.emit("vector", lambda v, sl=sl: v.tensor_add(
                    dst_b.ap[:, sl], dst_b.ap[:, sl], r_ps1.ap[:, :]),
                    reads=[r_ps1], writes=[dst_b], inc=cv)

        for tb in range(NTB):
            t0, t1 = tb * TB, (tb + 1) * TB
            xs, h1s = xs_p[tb % 2], h1_p[tb % 2]
            p.emit("gpsimd", lambda g, xs=xs, t0=t0, t1=t1: g.dma_start(
                out=xs.ap.rearrange("p (c j) -> p c j", c=DC),
                in_=xT.rearrange("(c p) t -> p c t", p=128)[:, :, t0:t1]),
                writes=[xs], inc=d_g)
            ln_stats_and_coeffs(xs)
            ln_apply(h1s, xs, rows1, rows1_b)
            for mc in range(9):
                w = 128 if mc < 8 else 2 * N_ST
                pj = (r_ps2, r_ps3)[mc % 2]
                for kc in range(DC):
                    for hh in range(2):
                        last = (kc == DC - 1 and hh == 1)
                        p.emit("tensor",
                               lambda t, mc=mc, kc=kc, w=w, pj=pj, hh=hh, h1s=h1s:
                               t.matmul(
                            pj.ap[0:w, hh * 512:(hh + 1) * 512],
                            big1[:, kc * 1056 + mc * 128:kc * 1056 + mc * 128 + w],
                            h1s.ap[:, kc * TB + hh * 512:kc * TB + (hh + 1) * 512],
                            start=(kc == 0), stop=(kc == DC - 1)),
                            reads=[wx_b, h1s], writes=[pj] if last else (),
                            war=() if (kc or hh) else [pj],
                            inc=ct if last else None)
                if mc < 8:
                    sl = slice(mc * TB, (mc + 1) * TB)
                    p.emit("scalar", lambda s, pj=pj, mc=mc: s.activation(
                        esb.ap, pj.ap, AF.Exp, bias=cons[:, 8 + mc:9 + mc]),
                        reads=[pj, bxpc_b], writes=[esb], inc=cs)
                    p.emit("scalar", lambda s, sl=sl: s.activation(
                        dls.ap[:, sl], esb.ap, AF.Ln, bias=1.0),
                        reads=[esb], writes=[dls], inc=cs)
                    p.emit("vector", lambda v, sl=sl, h1s=h1s: v.tensor_mul(
                        uss.ap[:, sl], dls.ap[:, sl], h1s.ap[:, sl]),
                        reads=[dls, h1s], writes=[uss], inc=cv)
                else:
                    p.emit("vector", lambda v, pj=pj, t0=t0, t1=t1:
                           v.tensor_single_scalar(
                        bc_sb[:, t0:t1], pj.ap[0:2 * N_ST, :],
                        cons[0:2 * N_ST, 24:25], ALU.add),
                        reads=[pj, bbc_b], writes=[bc_b], inc=cv)
            for (buf, scr, off) in ((h1s, h1_scr, 0), (dls, du_scr, 0),
                                    (uss, du_scr, L)):
                p.emit("sync", lambda s, buf=buf, scr=scr, t0=t0, t1=t1, off=off:
                       s.dma_start(
                    out=scr[:, :, off + t0:off + t1].transpose([1, 0, 2]),
                    in_=buf.ap.rearrange("p (c j) -> p c j", c=DC)),
                    reads=[buf], inc=d_w)

        p.barrier()
        y_dc = [Buf(big1[:, dc * L:(dc + 1) * L], f"y{dc}") for dc in range(DC)]
        brep = [Buf(big2[:, i * L:(i + 1) * L], f"brep{i}") for i in range(2)]
        crep = [Buf(big2[:, (2 + i) * L:(3 + i) * L], f"crep{i}") for i in range(2)]
        dustr = [Buf(big2[:, 4 * L:6 * L], "dustr")] * 2
        zring = [Buf(big2[:, (6 + i) * L:(7 + i) * L], f"zr{i}")
                 for i in range(2)]
        abar_t = Buf(big4[:, 0:L], "abar")
        abar = [abar_t, abar_t]
        bx_t = Buf(big4[:, L:2 * L], "bx")
        h_t = Buf(big4[:, 2 * L:3 * L], "h")
        brep_ps = Buf(ps_all[:, :], "brep_ps")

        for dc in range(DC):
            p.emit("vector", lambda v, dc=dc: v.memset(y_dc[dc].ap, 0.0),
                   writes=[y_dc[dc]], inc=cv)

        bcstg_b = [Buf(bcstg[0:1, 0:L], "bcstg"), Buf(bcstg[0:1, 0:L], "bcstg")]
        bcstg_b[1] = bcstg_b[0]
        for n in range(N_ST):
            bb, cb_ = brep[n % 2], crep[n % 2]
            for si, (rb, row) in enumerate(((bb, n), (cb_, N_ST + n))):
                stg = bcstg_b[si]
                p.emit("sync", lambda s, stg=stg, row=row: s.dma_start(
                    out=stg.ap, in_=bc_sb[row:row + 1, :]),
                    reads=[bc_b], writes=[stg], inc=d_s)
                for q in range(8):
                    last = (q == 7)
                    p.emit("tensor", lambda t, stg=stg, q=q: t.matmul(
                        brep_ps.ap[:, q * 512:(q + 1) * 512],
                        ones_r[0:1, :],
                        stg.ap[:, q * 512:(q + 1) * 512],
                        start=True, stop=True),
                        reads=[stg, onesr_b], writes=[brep_ps] if last else (),
                        war=() if q else [brep_ps],
                        inc=ct if last else None)
                p.emit("scalar", lambda s, rb=rb: s.activation(rb.ap, brep_ps.ap,
                                                               AF.Copy),
                       reads=[brep_ps], writes=[rb], inc=cs)
            for dc in range(DC):
                du = dustr[dc % 2]
                p.emit("sync", lambda s, du=du, dc=dc: s.dma_start(
                    out=du.ap, in_=du_scr[dc]), writes=[du], inc=d_s)
                ab = abar[dc % 2]
                p.emit("scalar", lambda s, ab=ab, du=du, dc=dc, n=n: s.activation(
                    ab.ap, du.ap[:, 0:L], AF.Exp,
                    scale=a_sc[:, dc * N_ST + n:dc * N_ST + n + 1]),
                    reads=[du, asc_b], writes=[ab], inc=cs)
                p.emit("vector", lambda v, du=du, bb=bb: v.tensor_mul(
                    bx_t.ap, du.ap[:, L:2 * L], bb.ap),
                    reads=[du, bb], writes=[bx_t], inc=cv)
                p.emit("vector", lambda v, ab=ab: v.tensor_tensor_scan(
                    h_t.ap, ab.ap, bx_t.ap, 0.0, ALU.mult, ALU.add),
                    reads=[ab, bx_t], writes=[h_t], inc=cv)
                zs = zring[(n * DC + dc) % 2]
                z_eng = "gpsimd" if ((n * DC + dc) % 4 == 1) else "vector"
                z_chan = cg if z_eng == "gpsimd" else cv
                p.emit(z_eng, lambda e, cb_=cb_, zs=zs: e.tensor_mul(
                    zs.ap, h_t.ap, cb_.ap),
                    reads=[h_t, cb_], writes=[zs], inc=z_chan)
                p.emit("gpsimd", lambda g, dc=dc, zs=zs: g.tensor_add(
                    y_dc[dc].ap, y_dc[dc].ap, zs.ap),
                    reads=[zs], writes=[y_dc[dc]], inc=cg)

        for dc in range(DC):
            du = dustr[0]
            hv = Buf(du.ap[:, 0:L], f"h1v{dc}")
            hv.last_write = du.last_write
            hv.readers = list(du.readers)
            p.emit("sync", lambda s, du=du, dc=dc: s.dma_start(
                out=du.ap[:, 0:L], in_=h1_scr[dc]),
                reads=[], writes=[du], inc=d_s)
            p.emit("vector", lambda v, du=du, dc=dc: v.scalar_tensor_tensor(
                y_dc[dc].ap, du.ap[:, 0:L], cons[:, dc:dc + 1], y_dc[dc].ap,
                ALU.mult, ALU.add),
                reads=[du, dp_b], writes=[y_dc[dc]], inc=cv)

        p.barrier()
        r1s = Buf(big2[:, 0:8 * TB], "r1")
        h2s = Buf(big2[:, 8 * TB:16 * TB], "h2")
        wcf = Buf(big2[:, 16 * TB:24 * TB], "wc_full")
        wsf = Buf(big2[:, 24 * TB:32 * TB], "ws_full")
        xb2 = Buf(big4[:, 0:8 * TB], "xb2")
        sqr = [Buf(big4[:, 8 * TB + i * TB:8 * TB + (i + 1) * TB], f"sqr{i}")
               for i in range(2)]
        sq1 = Buf(wcr[:].bitcast(F32), "sq1")
        sq2 = Buf(wsr[:].bitcast(F32), "sq2")
        omg = Buf(big4[:, 10 * TB:12 * TB].bitcast(F32), "omg")
        worb = [Buf(wor[:, i * DC * 128:(i + 1) * DC * 128], f"wor{i}") for i in range(2)]

        for (scr, buf) in ((wc_scr, wcf), (ws_scr, wsf)):
            p.emit("sync", lambda s, scr=scr, buf=buf: s.dma_start(
                out=buf.ap.rearrange("p (c j) -> p c j", c=DC),
                in_=scr.rearrange("(c p) j -> p c j", p=128)),
                writes=[buf], inc=d_s)

        def load_wring(scr, rb, mc):
            src = scr.rearrange("(c p) j -> p c j", p=128)[:, :, mc * 128:(mc + 1) * 128]
            p.emit("sync", lambda s: s.dma_start(
                out=rb.ap.rearrange("p (c j) -> p c j", c=DC), in_=src),
                writes=[rb], inc=d_s)

        for tb in range(NTB):
            t0, t1 = tb * TB, (tb + 1) * TB
            p.emit("gpsimd", lambda g, t0=t0, t1=t1: g.dma_start(
                out=xb2.ap.rearrange("p (c j) -> p c j", c=DC),
                in_=xT.rearrange("(c p) t -> p c t", p=128)[:, :, t0:t1]),
                writes=[xb2], inc=d_g)
            for mc in range(DC):
                rb = worb[mc % 2]
                load_wring(wo_scr, rb, mc)
                pj = (r_ps0, r_ps1)[mc % 2]
                for kc in range(DC):
                    for hh in range(2):
                        last = (kc == DC - 1 and hh == 1)
                        p.emit("tensor",
                               lambda t, rb=rb, kc=kc, pj=pj, t0=t0, hh=hh:
                               t.matmul(pj.ap[:, hh * 512:(hh + 1) * 512],
                                        rb.ap[:, kc * 128:(kc + 1) * 128],
                                        big1[:, kc * L + t0 + hh * 512:
                                             kc * L + t0 + (hh + 1) * 512],
                                        start=(kc == 0), stop=(kc == DC - 1)),
                               reads=[rb, y_dc[kc]], writes=[pj] if last else (),
                               war=() if (kc or hh) else [pj],
                               inc=ct if last else None)
                sl = slice(mc * TB, (mc + 1) * TB)
                p.emit("vector", lambda v, pj=pj, mc=mc, sl=sl: v.tensor_scalar(
                    h2s.ap[:, sl], pj.ap, cons[:, 16 + mc:17 + mc],
                    gcols[:, mc:mc + 1], ALU.add, ALU.mult),
                    reads=[pj, boutc_b, g_b], writes=[h2s], inc=cv)
                p.emit("vector", lambda v, sl=sl: v.tensor_add(
                    r1s.ap[:, sl], h2s.ap[:, sl], xb2.ap[:, sl]),
                    reads=[h2s, xb2], writes=[r1s], inc=cv)
            for dc in range(DC):
                for hh in range(2):
                    last = (dc == DC - 1 and hh == 1)
                    p.emit("tensor", lambda t, dc=dc, hh=hh: t.matmul(
                        r_ps2.ap[0:1, hh * 512:(hh + 1) * 512], ones1[:, 0:1],
                        r1s.ap[:, dc * TB + hh * 512:dc * TB + (hh + 1) * 512],
                        start=(dc == 0), stop=(dc == DC - 1)),
                        reads=[r1s, ones_b], writes=[r_ps2] if last else (),
                        war=() if (dc or hh) else [r_ps2],
                        inc=ct if last else None)
            for dc in range(DC):
                sb_ = sqr[dc % 2]
                sl = slice(dc * TB, (dc + 1) * TB)
                p.emit("vector", lambda v, sb_=sb_, sl=sl: v.tensor_mul(
                    sb_.ap, r1s.ap[:, sl], r1s.ap[:, sl]),
                    reads=[r1s], writes=[sb_], inc=cv)
                for hh in range(2):
                    last = (dc == DC - 1 and hh == 1)
                    p.emit("tensor", lambda t, sb_=sb_, dc=dc, hh=hh: t.matmul(
                        r_ps3.ap[0:1, hh * 512:(hh + 1) * 512], ones1[:, 0:1],
                        sb_.ap[:, hh * 512:(hh + 1) * 512],
                        start=(dc == 0), stop=(dc == DC - 1)),
                        reads=[sb_, ones_b], writes=[r_ps3] if last else (),
                        war=() if (dc or hh) else [r_ps3],
                        inc=ct if hh == 1 else None)
            p.emit("vector", lambda v: v.tensor_scalar_mul(rowf[0:1, 0:TB],
                                                           r_ps2.ap[0:1, :], 1.0 / D),
                   reads=[r_ps2], writes=[rowf_b], inc=cv)
            p.emit("scalar", lambda s: s.activation(rowf[0:1, TB:2 * TB], rowf[0:1, 0:TB],
                                                    AF.Square),
                   reads=[rowf_b], writes=[rowf_b], inc=cs)
            p.emit("vector", lambda v: v.tensor_scalar(rowf[0:1, 2 * TB:3 * TB],
                                                       r_ps3.ap[0:1, :], 1.0 / D,
                                                       EPS, ALU.mult, ALU.add),
                   reads=[r_ps3], writes=[rowf_b], inc=cv)
            p.emit("vector", lambda v: v.tensor_sub(rowf[0:1, 2 * TB:3 * TB], rowf[0:1, 2 * TB:3 * TB],
                                                    rowf[0:1, TB:2 * TB]),
                   reads=[rowf_b], writes=[rowf_b], inc=cv)
            p.emit("scalar", lambda s: s.activation(rowf[0:1, 2 * TB:3 * TB], rowf[0:1, 2 * TB:3 * TB],
                                                    AF.Sqrt),
                   reads=[rowf_b], writes=[rowf_b], inc=cs)
            p.emit("vector", lambda v: v.reciprocal(rowf[0:1, 2 * TB:3 * TB], rowf[0:1, 2 * TB:3 * TB]),
                   reads=[rowf_b], writes=[rowf_b], inc=cv)
            p.emit("vector", lambda v: v.tensor_copy(ab_a[0:1, :], rowf[0:1, 2 * TB:3 * TB]),
                   reads=[rowf_b], writes=[aba_b], inc=cv)
            p.emit("vector", lambda v: v.scalar_tensor_tensor(
                ab_bo[0:1, :], rowf[0:1, 0:TB], -1.0, rowf[0:1, 2 * TB:3 * TB],
                ALU.mult, ALU.mult),
                reads=[rowf_b], writes=[abbo_b], inc=cv)
            ln_apply(h2s, r1s, rows2, rows2_b)
            for mc in range(DC):
                for kc in range(DC):
                    for hh in range(2):
                        last = (kc == DC - 1 and hh == 1)
                        p.emit("tensor", lambda t, mc=mc, kc=kc, hh=hh: t.matmul(
                            r_ps0.ap[:, hh * 512:(hh + 1) * 512],
                            wcf.ap[:, kc * TB + mc * 128:kc * TB + mc * 128 + 128],
                            h2s.ap[:, kc * TB + hh * 512:kc * TB + (hh + 1) * 512],
                            start=(kc == 0), stop=(kc == DC - 1)),
                            reads=[wcf, h2s], writes=[r_ps0] if last else (),
                            war=() if (kc or hh) else [r_ps0],
                            inc=ct if last else None)
                for kc in range(DC):
                    for hh in range(2):
                        last = (kc == DC - 1 and hh == 1)
                        p.emit("tensor", lambda t, mc=mc, kc=kc, hh=hh: t.matmul(
                            r_ps1.ap[:, hh * 512:(hh + 1) * 512],
                            wsf.ap[:, kc * TB + mc * 128:kc * TB + mc * 128 + 128],
                            h2s.ap[:, kc * TB + hh * 512:kc * TB + (hh + 1) * 512],
                            start=(kc == 0), stop=(kc == DC - 1)),
                            reads=[wsf, h2s], writes=[r_ps1] if last else (),
                            war=() if (kc or hh) else [r_ps1],
                            inc=ct if last else None)
                p.emit("scalar", lambda s: s.activation(sq1.ap, r_ps0.ap, AF.Square),
                       reads=[r_ps0], writes=[sq1], inc=cs)
                p.emit("scalar", lambda s: s.activation(sq2.ap, r_ps1.ap, AF.Square),
                       reads=[r_ps1], writes=[sq2], inc=cs)
                p.emit("vector", lambda v: v.tensor_add(sq1.ap, sq1.ap, sq2.ap),
                       reads=[sq1, sq2], writes=[sq1], inc=cv)
                p.emit("scalar", lambda s: s.activation(sq2.ap, sq1.ap, AF.Sqrt),
                       reads=[sq1], writes=[sq2], inc=cs)
                sl = slice(mc * TB, (mc + 1) * TB)
                p.emit("vector", lambda v, mc=mc, sl=sl:
                       v.scalar_tensor_tensor(omg.ap, sq2.ap,
                                              gcols[:, DC + mc:DC + mc + 1],
                                              r1s.ap[:, sl], ALU.mult, ALU.add),
                       reads=[sq2, g_b, r1s], writes=[omg], inc=cv)
                p.emit("sync", lambda s, mc=mc, t0=t0, t1=t1: s.dma_start(
                    out=outT[mc * 128:(mc + 1) * 128, t0:t1], in_=omg.ap),
                    reads=[omg], inc=d_o)

        p.emit("sync", None, extra_waits=[d_o])
        p.build(block)
    return nc


_NC_CACHE = {}


def kernel(L0_attn_x, L0_attn_A_log, L0_attn_Dp, L0_attn_W_xproj, L0_attn_b_xproj,
           L0_attn_W_out, L0_attn_b_out, L0_ffn_theta, L0_ffn_phi,
           L0_ffn_theta_noise, L0_ffn_phi_noise, L0_c, L0_wmod):
    B = L0_attn_x.shape[0]
    f32 = np.float32
    cc = np.ascontiguousarray

    A_log_r = cc(np.asarray(L0_attn_A_log, f32).reshape(DC, 128, N_ST)
                 .transpose(1, 0, 2).reshape(128, DC * N_ST))
    Dp_c = cc(np.asarray(L0_attn_Dp, f32).reshape(DC, 128).T)
    bxp_c = cc(np.asarray(L0_attn_b_xproj[:D], f32).reshape(DC, 128).T)
    bbc_c = cc(np.asarray(L0_attn_b_xproj[D:], f32).reshape(2 * N_ST, 1))
    bout_c = cc(np.asarray(L0_attn_b_out, f32).reshape(DC, 128).T)

    if "nc" not in _NC_CACHE:
        _NC_CACHE["nc"] = build_program()
    nc = _NC_CACHE["nc"]

    in_maps = []
    for b in range(B):
        in_maps.append(dict(
            xT=cc(np.asarray(L0_attn_x[b], f32).T),
            A_log_r=A_log_r,
            Dp_c=Dp_c,
            Wxp=cc(np.asarray(L0_attn_W_xproj, f32)),
            bxp_c=bxp_c,
            bbc_c=bbc_c,
            Wout=cc(np.asarray(L0_attn_W_out, f32)),
            bout_c=bout_c,
            theta=cc(np.asarray(L0_ffn_theta, f32)),
            phi=cc(np.asarray(L0_ffn_phi, f32)),
            theta_n=cc(np.asarray(L0_ffn_theta_noise, f32)),
            phi_n=cc(np.asarray(L0_ffn_phi_noise, f32)),
            c_col=cc(np.asarray(L0_c[b], f32).reshape(DC, 128).T),
            wmod=cc(np.asarray(L0_wmod, f32)),
        ))

    res = run_bass_kernel_spmd(nc, in_maps, core_ids=list(range(B)))
    out = np.empty((B, L, D), np.float32)
    for b in range(B):
        out[b] = res.results[b]["outT"].T
    return out


# version 21 (final)
# speedup vs baseline: 1.4514x; 1.2101x over previous
import numpy as np
import ml_dtypes
from contextlib import ExitStack

import concourse.bass as bass
import concourse.mybir as mybir
from concourse.bass_utils import run_bass_kernel_spmd

F32 = mybir.dt.float32
BF16 = mybir.dt.bfloat16
AF = mybir.ActivationFunctionType
ALU = mybir.AluOpType

ENGINES = ("sync", "scalar", "vector", "tensor", "gpsimd")

D = 1024
L = 4096
N_ST = 16
DC = 8
TB = 1024
NTB = L // TB
EPS = 1e-5



class Chan:

    def __init__(self, handle, dma=False):
        self.h = handle
        self.dma = dma
        self.count = 0
        self.amt = 16 if dma else 1


class Buf:

    def __init__(self, ap, name=""):
        self.ap = ap
        self.name = name
        self.last_write = None
        self.readers = []


class Prog:
    def __init__(self, nc):
        self.nc = nc
        self.steps = {e: [] for e in ENGINES}
        self.hwm = {e: {} for e in ENGINES}
        self.all_chans = []

    def chan(self, es, name, dma=False):
        h = es.enter_context(self.nc.semaphore(name))
        c = Chan(h, dma=dma)
        self.all_chans.append(c)
        return c

    def emit(self, engine, fn, reads=(), writes=(), extra_waits=(), inc=None,
             war=()):
        need = {}

        def add(cc):
            if cc is None:
                return
            ch, v = cc
            if v <= 0:
                return
            if need.get(ch, 0) < v:
                need[ch] = v

        for b in reads:
            add(b.last_write)
        for b in list(writes) + list(war):
            add(b.last_write)
            for r in b.readers:
                add(r)
        for w in extra_waits:
            add(w if isinstance(w, tuple) else (w, w.count))

        if inc is not None and inc.dma:
            add((inc, inc.count))
        hw = self.hwm[engine]
        waits = []
        for ch, v in need.items():
            if hw.get(ch, 0) < v:
                waits.append((ch, v))
                hw[ch] = v
        if inc is not None:
            inc.count += inc.amt
        self.steps[engine].append((fn, waits, inc))
        tag = (inc, inc.count) if inc is not None else None
        for b in writes:
            b.last_write = tag
            b.readers = []
        if tag is not None:
            for b in reads:
                b.readers.append(tag)

    def barrier(self):
        for e in ENGINES:
            for ch in self.all_chans:
                if ch.count > 0 and self.hwm[e].get(ch, 0) < ch.count:
                    self.steps[e].append((None, [(ch, ch.count)], None))
                    self.hwm[e][ch] = ch.count

    def build(self, block):
        def mk(engine):
            def run(eng):
                for fn, waits, inc in self.steps[engine]:
                    for ch, v in waits:
                        eng.wait_ge(ch.h, v)
                    if fn is not None:
                        ins = fn(eng)
                        if inc is not None:
                            assert ins is not None
                            ins.then_inc(inc.h, inc.amt)
                    elif inc is not None:
                        eng.sem_inc(inc.h, inc.amt)
            return run
        block.sync(mk("sync"))
        block.scalar(mk("scalar"))
        block.vector(mk("vector"))
        block.tensor(mk("tensor"))
        block.gpsimd(mk("gpsimd"))



def build_program():
    nc = bass.Bass("TRN2", target_bir_lowering=False)

    xT = nc.dram_tensor("xT", [D, L], F32, kind="ExternalInput").ap()
    A_log_r = nc.dram_tensor("A_log_r", [128, DC * N_ST], F32, kind="ExternalInput").ap()
    Dp_c = nc.dram_tensor("Dp_c", [128, DC], F32, kind="ExternalInput").ap()
    Wxp = nc.dram_tensor("Wxp", [D, D + 2 * N_ST], F32, kind="ExternalInput").ap()
    bxp_c = nc.dram_tensor("bxp_c", [128, DC], F32, kind="ExternalInput").ap()
    bbc_c = nc.dram_tensor("bbc_c", [2 * N_ST, 1], F32, kind="ExternalInput").ap()
    Wout = nc.dram_tensor("Wout", [D, D], F32, kind="ExternalInput").ap()
    bout_c = nc.dram_tensor("bout_c", [128, DC], F32, kind="ExternalInput").ap()
    theta = nc.dram_tensor("theta", [D, D], F32, kind="ExternalInput").ap()
    phi = nc.dram_tensor("phi", [D, D], F32, kind="ExternalInput").ap()
    theta_n = nc.dram_tensor("theta_n", [D, D], F32, kind="ExternalInput").ap()
    phi_n = nc.dram_tensor("phi_n", [D, D], F32, kind="ExternalInput").ap()
    c_col = nc.dram_tensor("c_col", [128, DC], F32, kind="ExternalInput").ap()
    wmod = nc.dram_tensor("wmod", [D, 6 * D], F32, kind="ExternalInput").ap()

    outT = nc.dram_tensor("outT", [D, L], F32, kind="ExternalOutput").ap()

    h1_scr = nc.dram_tensor("h1_scr", [DC, 128, L], BF16).ap()
    du_scr = nc.dram_tensor("du_scr", [DC, 128, 2 * L], BF16).ap()
    wo_scr = nc.dram_tensor("wo_scr", [D, D], BF16).ap()
    wc_scr = nc.dram_tensor("wc_scr", [D, D], BF16).ap()
    ws_scr = nc.dram_tensor("ws_scr", [D, D], BF16).ap()

    es = ExitStack()
    with es:
        p = Prog(nc)

        def sb(name, shape, dt):
            return es.enter_context(nc.sbuf_tensor(name, shape, dt))

        big1 = sb("big1", [128, 8 * L], BF16)
        big2 = sb("big2", [128, 8 * L], BF16)
        big4 = sb("big4", [128, 3 * L], BF16)
        wor = sb("wor", [128, 2 * DC * 128], BF16)
        wcr = sb("wcr", [128, 2 * DC * 128], BF16)
        wsr = sb("wsr", [128, 2 * DC * 128], BF16)
        bc_sb = sb("bc_sb", [2 * N_ST, L], BF16)
        bcstg = sb("bcstg", [1, L], BF16)
        cons = sb("cons", [128, 64], F32)
        a_sc = sb("a_sc", [128, 128], F32)
        mc_adj = sb("mc_adj", [128, 32], BF16)
        rows1 = sb("rows1", [2, D], BF16)
        rows2 = sb("rows2", [2, D], BF16)
        ab_a = sb("ab_a", [1, TB], BF16)
        ab_bo = sb("ab_bo", [2, TB], BF16)
        rowf = sb("rowf", [1, 3 * TB], F32)
        ones1 = sb("ones1", [128, 1], BF16)
        ones_r = sb("ones_r", [1, 128], BF16)
        sio = sb("sio", [128, DC], F32)
        siob = sb("siob", [128, DC], BF16)
        gcols = sb("gcols", [128, 4 * DC], F32)
        mcols_f = sb("mcols_f", [128, 48], F32)

        ps_all = es.enter_context(nc.psum_tensor("ps_all", [128, 4 * TB], F32))

        d_g = p.chan(es, "d_g", dma=True)
        d_s = p.chan(es, "d_s", dma=True)
        d_w = p.chan(es, "d_w", dma=True)
        d_o = p.chan(es, "d_o", dma=True)
        cv = p.chan(es, "cv")
        cs = p.chan(es, "cs")
        ct = p.chan(es, "ct")
        cg = p.chan(es, "cg")

        block = es.enter_context(nc.Block())

        asc_b = Buf(a_sc[:], "a_sc")
        p.emit("sync", lambda s: s.dma_start(out=a_sc[:], in_=A_log_r),
               writes=[asc_b], inc=d_s)
        p.emit("scalar", lambda s: s.activation(a_sc[:], a_sc[:], AF.Exp),
               reads=[asc_b], writes=[asc_b], inc=cs)
        p.emit("vector", lambda v: v.tensor_scalar_mul(a_sc[:], a_sc[:], -1.0),
               reads=[asc_b], writes=[asc_b], inc=cv)

        dp_b = Buf(cons[:, 0:DC], "dp")
        bxpc_b = Buf(cons[:, 8:8 + DC], "bxp")
        boutc_b = Buf(cons[:, 16:16 + DC], "bout")
        bbc_b = Buf(cons[0:2 * N_ST, 24:25], "bbc")
        p.emit("sync", lambda s: s.dma_start(out=cons[:, 0:DC], in_=Dp_c),
               writes=[dp_b], inc=d_s)
        p.emit("sync", lambda s: s.dma_start(out=cons[:, 8:8 + DC], in_=bxp_c),
               writes=[bxpc_b], inc=d_s)
        p.emit("sync", lambda s: s.dma_start(out=cons[:, 16:16 + DC], in_=bout_c),
               writes=[boutc_b], inc=d_s)
        p.emit("sync", lambda s: s.dma_start(out=cons[0:2 * N_ST, 24:25], in_=bbc_c),
               writes=[bbc_b], inc=d_s)

        sio_b = Buf(sio[:], "sio")
        siob_b = Buf(siob[:], "siob")
        p.emit("sync", lambda s: s.dma_start(out=sio[:], in_=c_col),
               writes=[sio_b], inc=d_s)
        p.emit("scalar", lambda s: s.activation(sio[:], sio[:], AF.Silu),
               reads=[sio_b], writes=[sio_b], inc=cs)
        p.emit("vector", lambda v: v.tensor_copy(siob[:], sio[:]),
               reads=[sio_b], writes=[siob_b], inc=cv)

        stg_a = Buf(big2[:, 0:8 * D], "stg_a")
        stg_b = Buf(big2[:, 8 * D:16 * D], "stg_b")
        stg_c = Buf(big2[:, 16 * D:24 * D], "stg_c")
        stg_d = Buf(big2[:, 24 * D:32 * D], "stg_d")

        def cast_weight_to_scr(w_in, w_out, stg):
            src = w_in.rearrange("(c p) j -> p c j", p=128)
            dst = w_out.rearrange("(c p) j -> p c j", p=128)
            sview = stg.ap.rearrange("p (c j) -> p c j", c=DC)
            p.emit("gpsimd", lambda g: g.dma_start(out=sview, in_=src),
                   writes=[stg], inc=d_g)
            p.emit("sync", lambda s: s.dma_start(out=dst, in_=sview),
                   reads=[stg], inc=d_w)

        cast_weight_to_scr(Wout, wo_scr, stg_a)

        def build_trig(w1, w2, w_out, stg1, stg2, is_cos):
            src1 = w1.rearrange("(c p) j -> p c j", p=128)
            src2 = w2.rearrange("(c p) j -> p c j", p=128)
            s1 = stg1.ap
            s2 = stg2.ap
            v1 = s1.rearrange("p (c j) -> p c j", c=DC)
            v2 = s2.rearrange("p (c j) -> p c j", c=DC)
            p.emit("gpsimd", lambda g: g.dma_start(out=v1, in_=src1),
                   writes=[stg1], inc=d_g)
            p.emit("gpsimd", lambda g: g.dma_start(out=v2, in_=src2),
                   writes=[stg2], inc=d_g)
            p.emit("vector", lambda v: v.tensor_add(s1, s1, s2),
                   reads=[stg1, stg2], writes=[stg1], inc=cv)
            p.emit("vector", lambda v: v.tensor_mul(s2, s1, s1),
                   reads=[stg1], writes=[stg2], inc=cv)
            if is_cos:
                p.emit("vector", lambda v: v.tensor_scalar(s2, s2, 1.0 / 24.0, -0.5,
                                                           ALU.mult, ALU.add),
                       reads=[stg2], writes=[stg2], inc=cv)
                p.emit("vector", lambda v: v.tensor_mul(s2, s2, s1),
                       reads=[stg1, stg2], writes=[stg2], inc=cv)
                p.emit("vector", lambda v: v.tensor_mul(s2, s2, s1),
                       reads=[stg1, stg2], writes=[stg2], inc=cv)
                p.emit("vector", lambda v: v.tensor_scalar_add(s1, s2, 0.0),
                       reads=[stg2], writes=[stg1], inc=cv)
            dst = w_out.rearrange("(c p) j -> p c j", p=128)
            sview = s1.rearrange("p (c j) -> p c j", c=DC)
            p.emit("sync", lambda s: s.dma_start(out=dst, in_=sview),
                   reads=[stg1], inc=d_w)

        build_trig(theta, theta_n, wc_scr, stg_a, stg_b, True)

        srcp = phi.rearrange("(c p) j -> p c j", p=128)
        srcpn = phi_n.rearrange("(c p) j -> p c j", p=128)
        sA, sB = stg_c, stg_d
        vA = sA.ap.rearrange("p (c j) -> p c j", c=DC)
        vB = sB.ap.rearrange("p (c j) -> p c j", c=DC)
        p.emit("gpsimd", lambda g: g.dma_start(out=vA, in_=srcp), writes=[sA], inc=d_g)
        p.emit("gpsimd", lambda g: g.dma_start(out=vB, in_=srcpn), writes=[sB], inc=d_g)
        p.emit("vector", lambda v: v.tensor_add(sA.ap, sA.ap, sB.ap),
               reads=[sA, sB], writes=[sA], inc=cv)
        p.emit("vector", lambda v: v.tensor_mul(sB.ap, sA.ap, sA.ap),
               reads=[sA], writes=[sB], inc=cv)
        p.emit("vector", lambda v: v.tensor_scalar(sB.ap, sB.ap, 1.0 / 120.0,
                                                   -1.0 / 6.0, ALU.mult, ALU.add),
               reads=[sB], writes=[sB], inc=cv)
        p.emit("vector", lambda v: v.tensor_mul(sB.ap, sB.ap, sA.ap),
               reads=[sA, sB], writes=[sB], inc=cv)
        p.emit("vector", lambda v: v.tensor_mul(sB.ap, sB.ap, sA.ap),
               reads=[sA, sB], writes=[sB], inc=cv)
        p.emit("vector", lambda v: v.tensor_mul(sB.ap, sB.ap, sA.ap),
               reads=[sA, sB], writes=[sB], inc=cv)
        p.emit("vector", lambda v: v.tensor_add(sB.ap, sB.ap, sA.ap),
               reads=[sA, sB], writes=[sB], inc=cv)
        dstws = ws_scr.rearrange("(c p) j -> p c j", p=128)
        p.emit("sync", lambda s: s.dma_start(out=dstws, in_=vB), reads=[sB], inc=d_w)

        wm1 = Buf(big1[:, 0:6 * D], "wm1")
        wm2 = Buf(big1[:, 6 * D:12 * D], "wm2")
        mcol_ps = Buf(ps_all[:, 0:48], "mcol_ps")
        mcols_b = Buf(mcols_f[:], "mcols")
        p.emit("vector", lambda v: v.memset(mcols_f[:], 0.0),
               writes=[mcols_b], inc=cv)
        for kc in range(DC):
            wmb = (wm1, wm2)[kc % 2]
            src = wmod[kc * 128:(kc + 1) * 128, :]
            p.emit("gpsimd", lambda g, wmb=wmb, src=src:
                   g.dma_start(out=wmb.ap, in_=src), writes=[wmb], inc=d_g)
            for j in range(48):
                grp, dcj = j // DC, j % DC
                last = (j == 47)
                p.emit("tensor", lambda t, wmb=wmb, kc=kc, j=j, grp=grp, dcj=dcj:
                       t.matmul(mcol_ps.ap[:, j:j + 1],
                                wmb.ap[:, grp * D + dcj * 128:grp * D + (dcj + 1) * 128],
                                siob[:, kc:kc + 1],
                                start=True, stop=True),
                       reads=[wmb, siob_b], writes=[mcol_ps] if last else (),
                       war=() if j else [mcol_ps],
                       inc=ct if last else None)
            p.emit("vector", lambda v: v.tensor_add(mcols_f[:], mcols_f[:],
                                                    mcol_ps.ap),
                   reads=[mcol_ps], writes=[mcols_b], inc=cv)

        mc_b = Buf(mc_adj[:], "mc_adj")
        p.emit("vector", lambda v: v.tensor_scalar_add(
            mc_adj[:, 0:16:2], mcols_f[:, 8:16], 1.0),
            reads=[mcols_b], writes=[mc_b], inc=cv)
        p.emit("vector", lambda v: v.tensor_copy(mc_adj[:, 1:16:2], mcols_f[:, 0:8]),
               reads=[mcols_b], writes=[mc_b], inc=cv)
        p.emit("vector", lambda v: v.tensor_scalar_add(
            mc_adj[:, 16:32:2], mcols_f[:, 32:40], 1.0),
            reads=[mcols_b], writes=[mc_b], inc=cv)
        p.emit("vector", lambda v: v.tensor_copy(mc_adj[:, 17:32:2], mcols_f[:, 24:32]),
               reads=[mcols_b], writes=[mc_b], inc=cv)

        rows1_b = Buf(rows1[:], "rows1")
        rows2_b = Buf(rows2[:], "rows2")
        for ln_i, (rt, rb_) in enumerate(((rows1, rows1_b), (rows2, rows2_b))):
            for dc in range(DC):
                for r in range(2):
                    col = ln_i * 16 + 2 * dc + r
                    p.emit("sync", lambda s, rt=rt, r=r, dc=dc, col=col: s.dma_start(
                        out=rt[r:r + 1, dc * 128:(dc + 1) * 128],
                        in_=mc_adj[:, col:col + 1],
                        allow_slow_non_contiguous=True),
                        reads=[mc_b], writes=[rb_], inc=d_s)

        g_b = Buf(gcols[:], "gcols")
        for gi, base in enumerate((16, 40)):
            dst = gcols[:, gi * DC:(gi + 1) * DC]
            tmp = gcols[:, (2 + gi) * DC:(3 + gi) * DC]
            src = mcols_f[:, base:base + DC]
            p.emit("vector", lambda v, tmp=tmp, src=src:
                   v.tensor_scalar(tmp, src, 0.0, 2.0, ALU.is_ge, ALU.mult),
                   reads=[mcols_b], writes=[g_b], inc=cv)
            p.emit("vector", lambda v, tmp=tmp:
                   v.tensor_scalar_add(tmp, tmp, -1.0),
                   reads=[g_b], writes=[g_b], inc=cv)
            p.emit("scalar", lambda s, dst=dst, src=src:
                   s.activation(dst, src, AF.Abs),
                   reads=[mcols_b], writes=[g_b], inc=cs)
            p.emit("vector", lambda v, dst=dst:
                   v.tensor_scalar_max(dst, dst, 0.25),
                   reads=[g_b], writes=[g_b], inc=cv)
            p.emit("vector", lambda v, dst=dst, tmp=tmp:
                   v.tensor_mul(dst, dst, tmp),
                   reads=[g_b], writes=[g_b], inc=cv)

        ones_rf = sb("ones_rf", [1, 128], F32)
        ones1f = sb("ones1f", [128, 1], F32)
        cbcolb = sb("cbcolb", [128, DC], BF16)
        onesrf_b = Buf(ones_rf[:], "ones_rf")
        ones1f_b = Buf(ones1f[:], "ones1f")
        cbcolb_b = Buf(cbcolb[:], "cbcolb")
        p.emit("vector", lambda v: v.memset(ones_rf[:], 1.0), writes=[onesrf_b], inc=cv)
        p.emit("vector", lambda v: v.memset(ones1f[:], 1.0), writes=[ones1f_b], inc=cv)
        p.emit("vector", lambda v: v.tensor_copy(cbcolb[:], mcols_f[:, 32:40]),
               reads=[mcols_b], writes=[cbcolb_b], inc=cv)
        kcons_b = Buf(cons[0:1, 30:32], "kcons")
        for (lo, dst_col, scl) in ((32, 30, -1.0), (24, 31, 1.0)):
            p.emit("tensor", lambda t, lo=lo: t.matmul(
                ps_all[0:1, 0:8], ones1f[:, 0:1], mcols_f[:, lo:lo + 8],
                start=True, stop=True),
                reads=[mcols_b, ones1f_b], writes=[mcol_ps], war=[mcol_ps],
                inc=ct)
            p.emit("vector", lambda v, dst_col=dst_col: v.tensor_reduce(
                cons[0:1, dst_col:dst_col + 1], ps_all[0:1, 0:8],
                mybir.AxisListType.X, ALU.add),
                reads=[mcol_ps], writes=[kcons_b], inc=cv)
            if scl < 0:
                p.emit("vector", lambda v, dst_col=dst_col: v.tensor_scalar_mul(
                    cons[0:1, dst_col:dst_col + 1],
                    cons[0:1, dst_col:dst_col + 1], -1.0),
                    reads=[kcons_b], writes=[kcons_b], inc=cv)
        ones_b = Buf(ones1[:], "ones1")
        p.emit("vector", lambda v: v.memset(ones1[:], 1.0), writes=[ones_b], inc=cv)
        onesr_b = Buf(ones_r[:], "ones_r")
        p.emit("vector", lambda v: v.memset(ones_r[:], 1.0), writes=[onesr_b], inc=cv)
        aba_b = Buf(ab_a[:], "ab_a")
        abbo_b = Buf(ab_bo[:], "ab_bo")
        p.emit("vector", lambda v: v.memset(ab_bo[0:2, :], 1.0),
               writes=[abbo_b], inc=cv)

        p.barrier()
        wx_b = Buf(big1[:, 0:DC * 1056], "wx")
        wxv = big1[:, 0:DC * 1056].rearrange("p (c j) -> p c j", c=DC)
        p.emit("gpsimd", lambda g: g.dma_start(
            out=wxv, in_=Wxp.rearrange("(c p) j -> p c j", p=128)),
            writes=[wx_b], inc=d_g)

        WX_END = DC * 1056
        xs_p = [Buf(big2[:, 0:8 * TB], "x_bf0"),
                Buf(big1[:, WX_END:WX_END + 8 * TB], "x_bf1")]
        h1_p = [Buf(big2[:, 8 * TB:16 * TB], "h1_0"),
                Buf(big1[:, WX_END + 8 * TB:WX_END + 16 * TB], "h1_1")]
        dls = Buf(big2[:, 16 * TB:24 * TB], "dl")
        uss = Buf(big2[:, 24 * TB:32 * TB], "u")
        xsq = Buf(big4[:, 0:8 * TB], "xsq")
        esb = Buf(big4[:, 8 * TB:9 * TB], "esb")
        r_ps0 = Buf(ps_all[:, 0:TB], "ps0")
        r_ps1 = Buf(ps_all[:, TB:2 * TB], "ps1")
        r_ps2 = Buf(ps_all[:, 2 * TB:3 * TB], "ps2")
        r_ps3 = Buf(ps_all[:, 3 * TB:4 * TB], "ps3")
        rowf_b = Buf(rowf[:], "rowf")
        bc_b = Buf(bc_sb[:], "bc")

        def ln_stats_and_coeffs(src_b):
            p.emit("vector", lambda v: v.tensor_mul(xsq.ap, src_b.ap, src_b.ap),
                   reads=[src_b], writes=[xsq], inc=cv)
            for dc in range(DC):
                for hh in range(2):
                    last = (dc == DC - 1 and hh == 1)
                    p.emit("tensor", lambda t, dc=dc, hh=hh: t.matmul(
                        r_ps0.ap[0:1, hh * 512:(hh + 1) * 512], ones1[:, 0:1],
                        src_b.ap[:, dc * TB + hh * 512:dc * TB + (hh + 1) * 512],
                        start=(dc == 0), stop=(dc == DC - 1)),
                        reads=[src_b, ones_b], writes=[r_ps0] if last else (),
                        war=() if (dc or hh) else [r_ps0],
                        inc=ct if last else None)
            for dc in range(DC):
                for hh in range(2):
                    last = (dc == DC - 1 and hh == 1)
                    p.emit("tensor", lambda t, dc=dc, hh=hh: t.matmul(
                        r_ps1.ap[0:1, hh * 512:(hh + 1) * 512], ones1[:, 0:1],
                        xsq.ap[:, dc * TB + hh * 512:dc * TB + (hh + 1) * 512],
                        start=(dc == 0), stop=(dc == DC - 1)),
                        reads=[xsq, ones_b], writes=[r_ps1] if last else (),
                        war=() if (dc or hh) else [r_ps1],
                        inc=ct if last else None)
            p.emit("vector", lambda v: v.tensor_scalar_mul(rowf[0:1, 0:TB],
                                                           r_ps0.ap[0:1, :], 1.0 / D),
                   reads=[r_ps0], writes=[rowf_b], inc=cv)
            p.emit("scalar", lambda s: s.activation(rowf[0:1, TB:2 * TB], rowf[0:1, 0:TB],
                                                    AF.Square),
                   reads=[rowf_b], writes=[rowf_b], inc=cs)
            p.emit("vector", lambda v: v.tensor_scalar(rowf[0:1, 2 * TB:3 * TB],
                                                       r_ps1.ap[0:1, :], 1.0 / D,
                                                       EPS, ALU.mult, ALU.add),
                   reads=[r_ps1], writes=[rowf_b], inc=cv)
            p.emit("vector", lambda v: v.tensor_sub(rowf[0:1, 2 * TB:3 * TB], rowf[0:1, 2 * TB:3 * TB],
                                                    rowf[0:1, TB:2 * TB]),
                   reads=[rowf_b], writes=[rowf_b], inc=cv)
            p.emit("scalar", lambda s: s.activation(rowf[0:1, 2 * TB:3 * TB], rowf[0:1, 2 * TB:3 * TB],
                                                    AF.Ln),
                   reads=[rowf_b], writes=[rowf_b], inc=cs)
            p.emit("vector", lambda v: v.tensor_scalar_mul(rowf[0:1, 2 * TB:3 * TB],
                                                           rowf[0:1, 2 * TB:3 * TB], -0.5),
                   reads=[rowf_b], writes=[rowf_b], inc=cv)
            p.emit("scalar", lambda s: s.activation(rowf[0:1, 2 * TB:3 * TB], rowf[0:1, 2 * TB:3 * TB],
                                                    AF.Exp),
                   reads=[rowf_b], writes=[rowf_b], inc=cs)
            p.emit("vector", lambda v: v.tensor_copy(ab_a[0:1, :], rowf[0:1, 2 * TB:3 * TB]),
                   reads=[rowf_b], writes=[aba_b], inc=cv)
            p.emit("vector", lambda v: v.scalar_tensor_tensor(
                ab_bo[0:1, :], rowf[0:1, 0:TB], -1.0, rowf[0:1, 2 * TB:3 * TB],
                ALU.mult, ALU.mult),
                reads=[rowf_b], writes=[abbo_b], inc=cv)

        def ln_apply(dst_b, src_b, rows_t, rows_tb):
            for dc in range(DC):
                sl128 = slice(dc * 128, (dc + 1) * 128)
                for hh in range(2):
                    h5 = slice(hh * 512, (hh + 1) * 512)
                    p.emit("tensor", lambda t, sl128=sl128, rows_t=rows_t, h5=h5:
                           t.matmul(r_ps0.ap[:, h5], rows_t[0:1, sl128],
                                    ab_a[0:1, h5], start=True, stop=True),
                           reads=[rows_tb, aba_b], writes=[r_ps0], inc=ct)
                    p.emit("tensor", lambda t, sl128=sl128, rows_t=rows_t, h5=h5:
                           t.matmul(r_ps1.ap[:, h5], rows_t[0:2, sl128],
                                    ab_bo[0:2, h5], start=True, stop=True),
                           reads=[rows_tb, abbo_b], writes=[r_ps1], inc=ct)
                sl = slice(dc * TB, (dc + 1) * TB)
                p.emit("vector", lambda v, sl=sl: v.tensor_mul(
                    dst_b.ap[:, sl], src_b.ap[:, sl], r_ps0.ap[:, :]),
                    reads=[src_b, r_ps0], writes=[dst_b], inc=cv)
                p.emit("vector", lambda v, sl=sl: v.tensor_add(
                    dst_b.ap[:, sl], dst_b.ap[:, sl], r_ps1.ap[:, :]),
                    reads=[r_ps1], writes=[dst_b], inc=cv)

        for tb in range(NTB):
            t0, t1 = tb * TB, (tb + 1) * TB
            xs, h1s = xs_p[tb % 2], h1_p[tb % 2]
            p.emit("gpsimd", lambda g, xs=xs, t0=t0, t1=t1: g.dma_start(
                out=xs.ap.rearrange("p (c j) -> p c j", c=DC),
                in_=xT.rearrange("(c p) t -> p c t", p=128)[:, :, t0:t1]),
                writes=[xs], inc=d_g)
            ln_stats_and_coeffs(xs)
            ln_apply(h1s, xs, rows1, rows1_b)
            for mc in range(9):
                w = 128 if mc < 8 else 2 * N_ST
                pj = (r_ps2, r_ps3)[mc % 2]
                for kc in range(DC):
                    for hh in range(2):
                        last = (kc == DC - 1 and hh == 1)
                        p.emit("tensor",
                               lambda t, mc=mc, kc=kc, w=w, pj=pj, hh=hh, h1s=h1s:
                               t.matmul(
                            pj.ap[0:w, hh * 512:(hh + 1) * 512],
                            big1[:, kc * 1056 + mc * 128:kc * 1056 + mc * 128 + w],
                            h1s.ap[:, kc * TB + hh * 512:kc * TB + (hh + 1) * 512],
                            start=(kc == 0), stop=(kc == DC - 1)),
                            reads=[wx_b, h1s], writes=[pj] if last else (),
                            war=() if (kc or hh) else [pj],
                            inc=ct if last else None)
                if mc < 8:
                    sl = slice(mc * TB, (mc + 1) * TB)
                    p.emit("scalar", lambda s, pj=pj, mc=mc: s.activation(
                        esb.ap, pj.ap, AF.Exp, bias=cons[:, 8 + mc:9 + mc]),
                        reads=[pj, bxpc_b], writes=[esb], inc=cs)
                    p.emit("scalar", lambda s, sl=sl: s.activation(
                        dls.ap[:, sl], esb.ap, AF.Ln, bias=1.0),
                        reads=[esb], writes=[dls], inc=cs)
                    p.emit("vector", lambda v, sl=sl, h1s=h1s: v.tensor_mul(
                        uss.ap[:, sl], dls.ap[:, sl], h1s.ap[:, sl]),
                        reads=[dls, h1s], writes=[uss], inc=cv)
                else:
                    p.emit("vector", lambda v, pj=pj, t0=t0, t1=t1:
                           v.tensor_single_scalar(
                        bc_sb[:, t0:t1], pj.ap[0:2 * N_ST, :],
                        cons[0:2 * N_ST, 24:25], ALU.add),
                        reads=[pj, bbc_b], writes=[bc_b], inc=cv)
            for (buf, scr, off) in ((h1s, h1_scr, 0), (dls, du_scr, 0),
                                    (uss, du_scr, L)):
                p.emit("sync", lambda s, buf=buf, scr=scr, t0=t0, t1=t1, off=off:
                       s.dma_start(
                    out=scr[:, :, off + t0:off + t1].transpose([1, 0, 2]),
                    in_=buf.ap.rearrange("p (c j) -> p c j", c=DC)),
                    reads=[buf], inc=d_w)

        p.barrier()
        y_dc = [Buf(big1[:, dc * L:(dc + 1) * L], f"y{dc}") for dc in range(DC)]
        brep = [Buf(big2[:, i * L:(i + 1) * L], f"brep{i}") for i in range(2)]
        crep = [Buf(big2[:, (2 + i) * L:(3 + i) * L], f"crep{i}") for i in range(2)]
        dustr = [Buf(big2[:, (4 + 2 * i) * L:(6 + 2 * i) * L], f"dustr{i}")
                 for i in range(2)]
        zg_t = Buf(wor[:, 0:2048], "zg")
        abar_t = Buf(big4[:, 0:L], "abar")
        abar = [abar_t, abar_t]
        bx_t = Buf(big4[:, L:2 * L], "bx")
        h_t = Buf(big4[:, 2 * L:3 * L], "h")
        brep_ps = Buf(ps_all[:, :], "brep_ps")

        for dc in range(DC):
            p.emit("vector", lambda v, dc=dc: v.memset(y_dc[dc].ap, 0.0),
                   writes=[y_dc[dc]], inc=cv)

        bcstg_b = [Buf(bcstg[0:1, 0:L], "bcstg"), Buf(bcstg[0:1, 0:L], "bcstg")]
        bcstg_b[1] = bcstg_b[0]
        for n in range(N_ST):
            bb, cb_ = brep[n % 2], crep[n % 2]
            for si, (rb, row) in enumerate(((bb, n), (cb_, N_ST + n))):
                stg = bcstg_b[si]
                p.emit("sync", lambda s, stg=stg, row=row: s.dma_start(
                    out=stg.ap, in_=bc_sb[row:row + 1, :]),
                    reads=[bc_b], writes=[stg], inc=d_s)
                for q in range(8):
                    last = (q == 7)
                    p.emit("tensor", lambda t, stg=stg, q=q: t.matmul(
                        brep_ps.ap[:, q * 512:(q + 1) * 512],
                        ones_r[0:1, :],
                        stg.ap[:, q * 512:(q + 1) * 512],
                        start=True, stop=True),
                        reads=[stg, onesr_b], writes=[brep_ps] if last else (),
                        war=() if q else [brep_ps],
                        inc=ct if last else None)
                p.emit("scalar", lambda s, rb=rb: s.activation(rb.ap, brep_ps.ap,
                                                               AF.Copy),
                       reads=[brep_ps], writes=[rb], inc=cs)
            for dc in range(DC):
                du = dustr[dc % 2]
                p.emit("sync", lambda s, du=du, dc=dc: s.dma_start(
                    out=du.ap, in_=du_scr[dc]), writes=[du], inc=d_s)
                ab = abar[dc % 2]
                p.emit("scalar", lambda s, ab=ab, du=du, dc=dc, n=n: s.activation(
                    ab.ap, du.ap[:, 0:L], AF.Exp,
                    scale=a_sc[:, dc * N_ST + n:dc * N_ST + n + 1]),
                    reads=[du, asc_b], writes=[ab], inc=cs)
                p.emit("vector", lambda v, du=du, bb=bb: v.tensor_mul(
                    bx_t.ap, du.ap[:, L:2 * L], bb.ap),
                    reads=[du, bb], writes=[bx_t], inc=cv)
                p.emit("vector", lambda v, ab=ab: v.tensor_tensor_scan(
                    h_t.ap, ab.ap, bx_t.ap, 0.0, ALU.mult, ALU.add),
                    reads=[ab, bx_t], writes=[h_t], inc=cv)
                p.emit("vector", lambda v, cb_=cb_: v.tensor_mul(
                    bx_t.ap, h_t.ap, cb_.ap),
                    reads=[h_t, cb_], writes=[bx_t], inc=cv)
                p.emit("vector", lambda v, dc=dc: v.tensor_add(
                    y_dc[dc].ap, y_dc[dc].ap, bx_t.ap),
                    reads=[bx_t], writes=[y_dc[dc]], inc=cv)

        for dc in range(DC):
            du = dustr[dc % 2]
            hv = Buf(du.ap[:, 0:L], f"h1v{dc}")
            hv.last_write = du.last_write
            hv.readers = list(du.readers)
            p.emit("sync", lambda s, du=du, dc=dc: s.dma_start(
                out=du.ap[:, 0:L], in_=h1_scr[dc]),
                reads=[], writes=[du], inc=d_s)
            p.emit("vector", lambda v, du=du, dc=dc: v.scalar_tensor_tensor(
                y_dc[dc].ap, du.ap[:, 0:L], cons[:, dc:dc + 1], y_dc[dc].ap,
                ALU.mult, ALU.add),
                reads=[du, dp_b], writes=[y_dc[dc]], inc=cv)

        p.barrier()
        r1s = Buf(big2[:, 0:8 * TB], "r1")
        h2s = Buf(big2[:, 8 * TB:16 * TB], "h2")
        wcf = Buf(big2[:, 16 * TB:24 * TB], "wc_full")
        wsf = Buf(big2[:, 24 * TB:32 * TB], "ws_full")
        xb2 = Buf(big4[:, 0:8 * TB], "xb2")
        sqr = [Buf(big4[:, 8 * TB + i * TB:8 * TB + (i + 1) * TB], f"sqr{i}")
               for i in range(2)]
        sq1 = Buf(wcr[:].bitcast(F32), "sq1")
        sq2 = Buf(wsr[:].bitcast(F32), "sq2")
        omg = Buf(big4[:, 10 * TB:12 * TB].bitcast(F32), "omg")
        worb = [Buf(wor[:, i * DC * 128:(i + 1) * DC * 128], f"wor{i}") for i in range(2)]

        for (scr, buf) in ((wc_scr, wcf), (ws_scr, wsf)):
            p.emit("sync", lambda s, scr=scr, buf=buf: s.dma_start(
                out=buf.ap.rearrange("p (c j) -> p c j", c=DC),
                in_=scr.rearrange("(c p) j -> p c j", p=128)),
                writes=[buf], inc=d_s)

        def load_wring(scr, rb, mc):
            src = scr.rearrange("(c p) j -> p c j", p=128)[:, :, mc * 128:(mc + 1) * 128]
            p.emit("sync", lambda s: s.dma_start(
                out=rb.ap.rearrange("p (c j) -> p c j", c=DC), in_=src),
                writes=[rb], inc=d_s)

        for tb in range(NTB):
            t0, t1 = tb * TB, (tb + 1) * TB
            p.emit("gpsimd", lambda g, t0=t0, t1=t1: g.dma_start(
                out=xb2.ap.rearrange("p (c j) -> p c j", c=DC),
                in_=xT.rearrange("(c p) t -> p c t", p=128)[:, :, t0:t1]),
                writes=[xb2], inc=d_g)
            for mc in range(DC):
                rb = worb[mc % 2]
                load_wring(wo_scr, rb, mc)
                pj = (r_ps0, r_ps1)[mc % 2]
                for kc in range(DC):
                    for hh in range(2):
                        last = (kc == DC - 1 and hh == 1)
                        p.emit("tensor",
                               lambda t, rb=rb, kc=kc, pj=pj, t0=t0, hh=hh:
                               t.matmul(pj.ap[:, hh * 512:(hh + 1) * 512],
                                        rb.ap[:, kc * 128:(kc + 1) * 128],
                                        big1[:, kc * L + t0 + hh * 512:
                                             kc * L + t0 + (hh + 1) * 512],
                                        start=(kc == 0), stop=(kc == DC - 1)),
                               reads=[rb, y_dc[kc]], writes=[pj] if last else (),
                               war=() if (kc or hh) else [pj],
                               inc=ct if last else None)
                sl = slice(mc * TB, (mc + 1) * TB)
                p.emit("vector", lambda v, pj=pj, mc=mc, sl=sl: v.tensor_scalar(
                    h2s.ap[:, sl], pj.ap, cons[:, 16 + mc:17 + mc],
                    gcols[:, mc:mc + 1], ALU.add, ALU.mult),
                    reads=[pj, boutc_b, g_b], writes=[h2s], inc=cv)
                p.emit("vector", lambda v, sl=sl: v.tensor_add(
                    r1s.ap[:, sl], h2s.ap[:, sl], xb2.ap[:, sl]),
                    reads=[h2s, xb2], writes=[r1s], inc=cv)
            for dc in range(DC):
                for hh in range(2):
                    last = (dc == DC - 1 and hh == 1)
                    p.emit("tensor", lambda t, dc=dc, hh=hh: t.matmul(
                        r_ps2.ap[0:1, hh * 512:(hh + 1) * 512], ones1[:, 0:1],
                        r1s.ap[:, dc * TB + hh * 512:dc * TB + (hh + 1) * 512],
                        start=(dc == 0), stop=(dc == DC - 1)),
                        reads=[r1s, ones_b], writes=[r_ps2] if last else (),
                        war=() if (dc or hh) else [r_ps2],
                        inc=ct if last else None)
            for dc in range(DC):
                sb_ = sqr[dc % 2]
                sl = slice(dc * TB, (dc + 1) * TB)
                p.emit("vector", lambda v, sb_=sb_, sl=sl: v.tensor_mul(
                    sb_.ap, r1s.ap[:, sl], r1s.ap[:, sl]),
                    reads=[r1s], writes=[sb_], inc=cv)
                for hh in range(2):
                    last = (dc == DC - 1 and hh == 1)
                    p.emit("tensor", lambda t, sb_=sb_, dc=dc, hh=hh: t.matmul(
                        r_ps3.ap[0:1, hh * 512:(hh + 1) * 512], ones1[:, 0:1],
                        sb_.ap[:, hh * 512:(hh + 1) * 512],
                        start=(dc == 0), stop=(dc == DC - 1)),
                        reads=[sb_, ones_b], writes=[r_ps3] if last else (),
                        war=() if (dc or hh) else [r_ps3],
                        inc=ct if hh == 1 else None)
                for hh in range(2):
                    last = (dc == DC - 1 and hh == 1)
                    p.emit("tensor", lambda t, dc=dc, hh=hh, sl=sl: t.matmul(
                        r_ps2.ap[32:33, hh * 512:(hh + 1) * 512],
                        cbcolb[:, dc:dc + 1],
                        r1s.ap[:, dc * TB + hh * 512:dc * TB + (hh + 1) * 512],
                        start=(dc == 0), stop=(dc == DC - 1)),
                        reads=[r1s, cbcolb_b], writes=[r_ps2] if last else (),
                        inc=ct if last else None)
            p.emit("vector", lambda v: v.tensor_scalar_mul(rowf[0:1, 0:TB],
                                                           r_ps2.ap[0:1, :], 1.0 / D),
                   reads=[r_ps2], writes=[rowf_b], inc=cv)
            p.emit("scalar", lambda s: s.activation(rowf[0:1, TB:2 * TB], rowf[0:1, 0:TB],
                                                    AF.Square),
                   reads=[rowf_b], writes=[rowf_b], inc=cs)
            p.emit("vector", lambda v: v.tensor_scalar(rowf[0:1, 2 * TB:3 * TB],
                                                       r_ps3.ap[0:1, :], 1.0 / D,
                                                       EPS, ALU.mult, ALU.add),
                   reads=[r_ps3], writes=[rowf_b], inc=cv)
            p.emit("vector", lambda v: v.tensor_sub(rowf[0:1, 2 * TB:3 * TB], rowf[0:1, 2 * TB:3 * TB],
                                                    rowf[0:1, TB:2 * TB]),
                   reads=[rowf_b], writes=[rowf_b], inc=cv)
            p.emit("scalar", lambda s: s.activation(rowf[0:1, 2 * TB:3 * TB], rowf[0:1, 2 * TB:3 * TB],
                                                    AF.Sqrt),
                   reads=[rowf_b], writes=[rowf_b], inc=cs)
            p.emit("vector", lambda v: v.reciprocal(rowf[0:1, 2 * TB:3 * TB], rowf[0:1, 2 * TB:3 * TB]),
                   reads=[rowf_b], writes=[rowf_b], inc=cv)
            p.emit("vector", lambda v: v.scalar_tensor_tensor(
                rowf[0:1, TB:2 * TB], rowf[0:1, 0:TB], cons[0:1, 30:31],
                r_ps2.ap[32:33, :], ALU.mult, ALU.add),
                reads=[rowf_b, kcons_b, r_ps2], writes=[rowf_b], inc=cv)
            p.emit("vector", lambda v: v.tensor_mul(
                rowf[0:1, TB:2 * TB], rowf[0:1, TB:2 * TB],
                rowf[0:1, 2 * TB:3 * TB]),
                reads=[rowf_b], writes=[rowf_b], inc=cv)
            p.emit("vector", lambda v: v.tensor_single_scalar(
                rowf[0:1, TB:2 * TB], rowf[0:1, TB:2 * TB],
                cons[0:1, 31:32], ALU.add),
                reads=[kcons_b], writes=[rowf_b], inc=cv)
            p.emit("vector", lambda v: v.tensor_copy(ab_a[0:1, :], rowf[0:1, 2 * TB:3 * TB]),
                   reads=[rowf_b], writes=[aba_b], inc=cv)
            p.emit("vector", lambda v: v.scalar_tensor_tensor(
                ab_bo[0:1, :], rowf[0:1, 0:TB], -1.0, rowf[0:1, 2 * TB:3 * TB],
                ALU.mult, ALU.mult),
                reads=[rowf_b], writes=[abbo_b], inc=cv)
            ln_apply(h2s, r1s, rows2, rows2_b)
            for mc in range(DC):
                for hh in range(2):
                    p.emit("tensor", lambda t, hh=hh: t.matmul(
                        r_ps0.ap[:, hh * 512:(hh + 1) * 512], ones_rf[0:1, :],
                        rowf[0:1, TB + hh * 512:TB + (hh + 1) * 512],
                        start=True, stop=False),
                        reads=[rowf_b, onesrf_b],
                        war=[r_ps0] if hh == 0 else (), inc=None)
                for kc in range(DC):
                    for hh in range(2):
                        last = (kc == DC - 1 and hh == 1)
                        p.emit("tensor", lambda t, mc=mc, kc=kc, hh=hh: t.matmul(
                            r_ps0.ap[:, hh * 512:(hh + 1) * 512],
                            wcf.ap[:, kc * TB + mc * 128:kc * TB + mc * 128 + 128],
                            h2s.ap[:, kc * TB + hh * 512:kc * TB + (hh + 1) * 512],
                            start=False, stop=(kc == DC - 1)),
                            reads=[wcf, h2s], writes=[r_ps0] if last else (),
                            inc=ct if last else None)
                for kc in range(DC):
                    for hh in range(2):
                        last = (kc == DC - 1 and hh == 1)
                        p.emit("tensor", lambda t, mc=mc, kc=kc, hh=hh: t.matmul(
                            r_ps1.ap[:, hh * 512:(hh + 1) * 512],
                            wsf.ap[:, kc * TB + mc * 128:kc * TB + mc * 128 + 128],
                            h2s.ap[:, kc * TB + hh * 512:kc * TB + (hh + 1) * 512],
                            start=(kc == 0), stop=(kc == DC - 1)),
                            reads=[wsf, h2s], writes=[r_ps1] if last else (),
                            war=() if (kc or hh) else [r_ps1],
                            inc=ct if last else None)
                p.emit("scalar", lambda s: s.activation(sq1.ap, r_ps0.ap, AF.Square),
                       reads=[r_ps0], writes=[sq1], inc=cs)
                p.emit("scalar", lambda s: s.activation(sq2.ap, r_ps1.ap, AF.Square),
                       reads=[r_ps1], writes=[sq2], inc=cs)
                p.emit("vector", lambda v: v.tensor_add(sq1.ap, sq1.ap, sq2.ap),
                       reads=[sq1, sq2], writes=[sq1], inc=cv)
                p.emit("scalar", lambda s: s.activation(sq2.ap, sq1.ap, AF.Sqrt),
                       reads=[sq1], writes=[sq2], inc=cs)
                sl = slice(mc * TB, (mc + 1) * TB)
                p.emit("vector", lambda v, mc=mc, sl=sl:
                       v.scalar_tensor_tensor(omg.ap, sq2.ap,
                                              gcols[:, DC + mc:DC + mc + 1],
                                              r1s.ap[:, sl], ALU.mult, ALU.add),
                       reads=[sq2, g_b, r1s], writes=[omg], inc=cv)
                p.emit("sync", lambda s, mc=mc, t0=t0, t1=t1: s.dma_start(
                    out=outT[mc * 128:(mc + 1) * 128, t0:t1], in_=omg.ap),
                    reads=[omg], inc=d_o)

        p.emit("sync", None, extra_waits=[d_o])
        p.build(block)
    return nc


_NC_CACHE = {}


def kernel(L0_attn_x, L0_attn_A_log, L0_attn_Dp, L0_attn_W_xproj, L0_attn_b_xproj,
           L0_attn_W_out, L0_attn_b_out, L0_ffn_theta, L0_ffn_phi,
           L0_ffn_theta_noise, L0_ffn_phi_noise, L0_c, L0_wmod):
    B = L0_attn_x.shape[0]
    f32 = np.float32
    cc = np.ascontiguousarray

    A_log_r = cc(np.asarray(L0_attn_A_log, f32).reshape(DC, 128, N_ST)
                 .transpose(1, 0, 2).reshape(128, DC * N_ST))
    Dp_c = cc(np.asarray(L0_attn_Dp, f32).reshape(DC, 128).T)
    bxp_c = cc(np.asarray(L0_attn_b_xproj[:D], f32).reshape(DC, 128).T)
    bbc_c = cc(np.asarray(L0_attn_b_xproj[D:], f32).reshape(2 * N_ST, 1))
    bout_c = cc(np.asarray(L0_attn_b_out, f32).reshape(DC, 128).T)

    if "nc" not in _NC_CACHE:
        _NC_CACHE["nc"] = build_program()
    nc = _NC_CACHE["nc"]

    in_maps = []
    for b in range(B):
        in_maps.append(dict(
            xT=cc(np.asarray(L0_attn_x[b], f32).T),
            A_log_r=A_log_r,
            Dp_c=Dp_c,
            Wxp=cc(np.asarray(L0_attn_W_xproj, f32)),
            bxp_c=bxp_c,
            bbc_c=bbc_c,
            Wout=cc(np.asarray(L0_attn_W_out, f32)),
            bout_c=bout_c,
            theta=cc(np.asarray(L0_ffn_theta, f32)),
            phi=cc(np.asarray(L0_ffn_phi, f32)),
            theta_n=cc(np.asarray(L0_ffn_theta_noise, f32)),
            phi_n=cc(np.asarray(L0_ffn_phi_noise, f32)),
            c_col=cc(np.asarray(L0_c[b], f32).reshape(DC, 128).T),
            wmod=cc(np.asarray(L0_wmod, f32)),
        ))

    res = run_bass_kernel_spmd(nc, in_maps, core_ids=list(range(B)))
    out = np.empty((B, L, D), np.float32)
    for b in range(B):
        out[b] = res.results[b]["outT"].T
    return out
